# version 09
# baseline (speedup 1.0000x reference)
import numpy as np
import ml_dtypes
from contextlib import ExitStack

import concourse.bass as bass
import concourse.mybir as mybir
from concourse.bass_utils import run_bass_kernel_spmd

F32 = mybir.dt.float32
BF16 = mybir.dt.bfloat16
AF = mybir.ActivationFunctionType
ALU = mybir.AluOpType
AX = mybir.AxisListType

T = 2048
D = 1024
S = 8192
CH = 8
NTT = 16
TGN, TGS = 4, 512
NQT, QTS = 8, 256
NKB, KBS = 16, 512
EPS = 1e-5

ENGINES = ("sync", "gpsimd", "tensor", "vector", "scalar")
CC_BARRIER = True

RW_M, RW_V, RW_A, RW_B = 0, 512, 1024, 1536
RW_MISC = 2048
RW_STG = 2112
RW2_M, RW2_V, RW2_A, RW2_B = 2112, 2368, 2624, 2880
ROWS_COLS = 3136

BC_BQ, BC_BK, BC_BV, BC_BP = 0, 8, 16, 24
BC_CB1, BC_CB2, BC_CB3, BC_CB4 = 32, 40, 44, 46
BC_GB1, BC_GB2, BC_GB3, BC_AR = 47, 51, 53, 54
BC_T1, BC_SADD, BC_SMUL, BC_GF = 55, 56, 57, 58

TWIST = {0: (4, +1), 1: (5, +1), 2: (6, -1), 3: (7, -1),
         4: (0, -1), 5: (1, -1), 6: (2, +1), 7: (3, +1)}


class Prog:

    def __init__(self):
        self.ops = {e: [] for e in ENGINES}
        self.cnt = {}
        self.waited = {e: {} for e in ENGINES}

    def _wait(self, eng, tok):
        if tok is None:
            return
        s, v = tok
        if v <= 0 or self.waited[eng].get(s, 0) >= v:
            return
        self.waited[eng][s] = v
        self.ops[eng].append(("w", s, v))

    def run(self, eng, build, waits=(), inc=None, inck=1):
        for tok in waits:
            self._wait(eng, tok)
        if inc is not None:
            self.cnt[inc] = self.cnt.get(inc, 0) + inck
            tok = (inc, self.cnt[inc])
            self.ops[eng].append(("i", build, inc, inck))
            return tok
        self.ops[eng].append(("r", build))
        return None

    def now(self, sem):
        return (sem, self.cnt.get(sem, 0))

    def emit(self, eng_obj, eng_name, handles):
        for op in self.ops[eng_name]:
            if op[0] == "w":
                eng_obj.wait_ge(handles[op[1]], op[2])
            elif op[0] == "i":
                op[1](eng_obj).then_inc(handles[op[2]], op[3])
            else:
                op[1](eng_obj)


def build(taps=(), upto=None, skip_cc=False):
    nc = bass.Bass("TRN2", target_bir_lowering=False)
    taps = set(taps)

    dx = nc.dram_tensor("x", [T, D], F32, kind="ExternalInput")
    dwqkv = nc.dram_tensor("w_qkv", [D, 3 * D], F32, kind="ExternalInput")
    dbqkv = nc.dram_tensor("b_qkv", [3 * D], F32, kind="ExternalInput")
    dwproj = nc.dram_tensor("w_proj", [D, D], F32, kind="ExternalInput")
    dbproj = nc.dram_tensor("b_proj", [D], F32, kind="ExternalInput")
    dcw1 = nc.dram_tensor("cn_w1", [D, D], F32, kind="ExternalInput")
    dcb1 = nc.dram_tensor("cn_b1", [D], F32, kind="ExternalInput")
    dcw2 = nc.dram_tensor("cn_w2", [D, 512], F32, kind="ExternalInput")
    dcb2 = nc.dram_tensor("cn_b2", [512], F32, kind="ExternalInput")
    dcw3 = nc.dram_tensor("cn_w3", [512, 256], F32, kind="ExternalInput")
    dcb3 = nc.dram_tensor("cn_b3", [256], F32, kind="ExternalInput")
    dcw4 = nc.dram_tensor("cn_w4", [256, 1], F32, kind="ExternalInput")
    dcb4 = nc.dram_tensor("cn_b4", [1], F32, kind="ExternalInput")
    dgw1 = nc.dram_tensor("gc_w1", [D, 512], F32, kind="ExternalInput")
    dgb1 = nc.dram_tensor("gc_b1", [512], F32, kind="ExternalInput")
    dgw2 = nc.dram_tensor("gc_w2", [512, 256], F32, kind="ExternalInput")
    dgb2 = nc.dram_tensor("gc_b2", [256], F32, kind="ExternalInput")
    dgw3 = nc.dram_tensor("gc_w3", [256, 1], F32, kind="ExternalInput")
    dgb3 = nc.dram_tensor("gc_b3", [1], F32, kind="ExternalInput")
    dar = nc.dram_tensor("adaptive_range", [1], F32, kind="ExternalInput")
    dy = nc.dram_tensor("y", [T, D], F32, kind="ExternalOutput")

    qt_dram = nc.dram_tensor("qt_dram", [D, T], BF16)
    kvag_in_t = nc.dram_tensor("kvag_in", [2 * D * T], BF16)
    kvag_out_t = nc.dram_tensor("kvag_out", [4 * 2 * D * T], BF16)
    kag_in = kvag_in_t[0: D * T].rearrange("(d t) -> d t", t=T)
    vag_in = kvag_in_t[D * T: 2 * D * T].rearrange("(t d) -> t d", d=D)
    def kag_out_r(r):
        return kvag_out_t[r * 2 * D * T: r * 2 * D * T + D * T].rearrange(
            "(d t) -> d t", t=T)
    def vag_out_r(r):
        return kvag_out_t[r * 2 * D * T + D * T: (r + 1) * 2 * D * T].rearrange(
            "(t d) -> t d", d=D)
    kmax_in = nc.dram_tensor("kmax_in", [1, 8], F32)
    kmax_out = nc.dram_tensor("kmax_out", [1, 8], F32)
    ctx_in = nc.dram_tensor("ctx_in", [128, CH], F32)
    ctx_out = nc.dram_tensor("ctx_out", [128, CH], F32)
    x1_dram = nc.dram_tensor("x1_dram", [D, T], F32)
    xfm_dram = nc.dram_tensor("xfm_dram", [D, T], F32)

    tap_handles = {}
    def add_tap(name, shape, dtype):
        if name in taps:
            tap_handles[name] = nc.dram_tensor("tap_" + name, shape, dtype,
                                               kind="ExternalOutput")
    add_tap("h1", [D, T], BF16)
    add_tap("kag", [D, T], BF16)
    add_tap("vag", [T, D], BF16)
    add_tap("qt", [D, T], BF16)
    add_tap("x1", [D, T], F32)
    add_tap("h2", [D, T], BF16)
    add_tap("ffn", [D, T], F32)
    add_tap("ctx", [128, CH], F32)
    add_tap("x1d", [D, T], F32)
    add_tap("yfm", [D, T], F32)

    st = ExitStack()
    def sbt(name, shape, dtype):
        return st.enter_context(nc.sbuf_tensor(name, shape, dtype))
    def pst(name, shape):
        return st.enter_context(nc.psum_tensor(name, shape, F32))

    stream = sbt("stream", [128, CH * T], F32)
    temps = sbt("temps", [128, 2048], F32)
    rows = sbt("rows", [128, ROWS_COLS], F32)
    pads = sbt("pads", [128, 512], F32)
    attnf = sbt("attnf", [128, 2048], F32)
    biasar = sbt("biasar", [128, 64], F32)
    e0f = sbt("e0f", [128, 128], F32)
    identf = sbt("identf", [128, 128], F32)
    ctxP = sbt("ctxP", [128, 16], F32)
    cst = sbt("cst", [128, 2], F32)

    xbA = sbt("xbA", [128, CH * T], BF16)
    wqA = sbt("wqA", [128, 16384], BF16)
    kvA = sbt("kvA", [128, 12288], BF16)
    prA = sbt("prA", [128, 1024], BF16)
    emA = sbt("emA", [128, 2048], BF16)
    stgA = sbt("stgA", [128, 2048], BF16)
    padb = sbt("padb", [128, 2048], BF16)
    e0b = sbt("e0b", [128, 128], BF16)
    onespadb = sbt("onespadb", [128, 128], BF16)
    meanpadb = sbt("meanpadb", [128, 128], BF16)
    qsqB = sbt("qsqB", [128, 2048], BF16)
    tfB = sbt("tfB", [128, 2048], BF16)

    PS0 = pst("PS0", [128, 2048])
    PS1 = pst("PS1", [128, 512])
    PS2 = pst("PS2", [128, 512])
    PS3 = pst("PS3", [128, 512])
    PS4 = pst("PS4", [128, 512])

    sems = ["sd", "sg", "so", "sp", "sv", "sa", "sc", "sb", "sk", "st",
            "sx0", "sx1", "ss0", "ss1", "kk0", "kk1", "kv0", "sq0", "sxsp",
            "xr0", "xr1", "xr2", "xr3", "sy0", "sy1", "w0", "w1", "w2", "w3",
            "xq0", "xq1", "x1q0", "x1q1", "sxf", "qld", "cs0", "cs1"]
    SYNC_RINGS = ("sd", "sb", "st", "sx0", "sx1", "ss0", "ss1", "kk0", "kk1",
                  "kv0", "sxsp", "xr0", "xr1", "xr2", "xr3", "sy0", "sy1", "xq0", "xq1", "x1q0", "x1q1", "sxf", "qld", "cs0", "cs1")
    GP_RINGS = ("sg", "sk", "sq0", "w0", "w1", "w2", "w3")
    handles = {s: st.enter_context(nc.semaphore(s)) for s in sems}

    nc.const_aps.register(F32, 0.0, cst[:, 0:1])
    nc.const_aps.register(F32, EPS, cst[:, 1:2])

    p = Prog()

    def chunk(c, lo=0, hi=T):
        return stream[:, c * T + lo: c * T + hi]

    def xb(c, lo=0, hi=T):
        return xbA[:, c * T + lo: c * T + hi]

    def rw(col, n):
        return rows[0:1, col:col + n]

    def bcol(c0, p0=0, p1=128):
        return biasar[p0:p1, c0:c0 + 1]

    def dma(eng, out, in_, waits=(), ring=None):
        sem = ring if ring is not None else ("sd" if eng == "sync" else "sg")
        return p.run(eng, lambda e, o=out, i=in_: e.dma_start(out=o, in_=i),
                     waits=waits, inc=sem, inck=16)

    def now_dma(eng):
        rings = SYNC_RINGS if eng == "sync" else GP_RINGS
        return tuple(p.now(s) for s in rings)

    def mm(out, lhsT, rhs, start, stop, waits=(), inc=None):
        return p.run("tensor",
                     lambda e, o=out, l=lhsT, r=rhs, s1=start, s2=stop:
                     e.matmul(o, l, r, start=s1, stop=s2, skip_group_check=True),
                     waits=waits, inc=inc)

    def act(out, in_, func, bias=0.0, scale=1.0, waits=(), inc="sa"):
        return p.run("scalar",
                     lambda e, o=out, i=in_, f=func, b=bias, s=scale:
                     e.activation(out=o, in_=i, func=f, bias=b, scale=s),
                     waits=waits, inc=inc)

    def tt(out, a, b, op, waits=(), inc="sv"):
        return p.run("vector",
                     lambda e, o=out, x=a, y=b, z=op: e.tensor_tensor(o, x, y, z),
                     waits=waits, inc=inc)

    def ts2(out, in_, s1, op0, s2=None, op1=None, waits=(), inc="sv"):
        if op1 is None:
            fn = lambda e, o=out, i=in_, a=s1, p0=op0: \
                e.tensor_scalar(out=o, in0=i, scalar1=a, scalar2=None, op0=p0)
        else:
            fn = lambda e, o=out, i=in_, a=s1, b=s2, p0=op0, p1=op1: \
                e.tensor_scalar(out=o, in0=i, scalar1=a, scalar2=b, op0=p0, op1=p1)
        return p.run("vector", fn, waits=waits, inc=inc)

    def vcopy(out, in_, waits=(), inc="sv"):
        return p.run("vector", lambda e, o=out, i=in_: e.tensor_copy(o, i),
                     waits=waits, inc=inc)

    def vrecip(out, in_, waits=()):
        return p.run("vector", lambda e, o=out, i=in_: e.reciprocal(o, i),
                     waits=waits, inc="sv")

    def gp(build_fn, waits=(), inc=None, inck=1):
        return p.run("gpsimd", build_fn, waits=waits, inc=inc, inck=inck)

    last_pad_mm = [None, None]
    def expand(row_ap, pb, n, waits=()):
        wv = vcopy(pads[0:1, 0:n], row_ap,
                   waits=tuple(waits) + (last_pad_mm[0], last_pad_mm[1]))
        tok = mm(pb[:, 0:n], e0f[:], pads[:, 0:n], True, True,
                 waits=(wv,), inc="sp")
        last_pad_mm[0] = tok
        last_pad_mm[1] = tok
        return tok

    gp(lambda e: e.memset(cst[:, 0:1], 0.0))
    gp(lambda e: e.memset(cst[:, 1:2], EPS))
    gp(lambda e: e.memset(pads[:], 0.0))
    gp(lambda e: e.memset(padb[:], 0.0))
    m1_ = gp(lambda e: e.memset(e0f[:], 0.0), inc="so", inck=1)
    m2_ = gp(lambda e: e.memset(e0b[:], 0.0), inc="so", inck=1)
    m3_ = gp(lambda e: e.memset(onespadb[:], 0.0), inc="so", inck=1)
    m4_ = gp(lambda e: e.memset(meanpadb[:], 0.0), inc="so", inck=1)
    m5_ = gp(lambda e: e.memset(identf[:], 0.0), inc="so", inck=1)
    gp(lambda e: e.affine_select(out=e0f[:], in_=e0f[:], compare_op=ALU.not_equal,
                                 fill=1.0, base=0, pattern=[[0, 128]], channel_multiplier=1),
       waits=(m5_,))
    gp(lambda e: e.affine_select(out=e0b[:], in_=e0b[:], compare_op=ALU.not_equal,
                                 fill=1.0, base=0, pattern=[[0, 128]], channel_multiplier=1),
       waits=(m2_,))
    gp(lambda e: e.affine_select(out=onespadb[:], in_=onespadb[:], compare_op=ALU.not_equal,
                                 fill=1.0, base=0, pattern=[[1, 128]], channel_multiplier=0),
       waits=(m3_,))
    gp(lambda e: e.affine_select(out=meanpadb[:], in_=meanpadb[:], compare_op=ALU.not_equal,
                                 fill=1.0 / D, base=0, pattern=[[1, 128]], channel_multiplier=0),
       waits=(m4_,))
    tok_consts = gp(lambda e: e.affine_select(out=identf[:], in_=identf[:],
                                              compare_op=ALU.not_equal,
                                              fill=1.0, base=0, pattern=[[-1, 128]],
                                              channel_multiplier=1),
                    waits=(m1_,), inc="so", inck=1)

    m6_ = gp(lambda e: e.memset(attnf[:, 0:128], 0.0), inc="so", inck=1)
    dma("sync", attnf[0:24, 0:128], dbqkv[:].rearrange("(a b) -> a b", b=128),
        waits=(m6_,), ring="sb")
    dma("sync", attnf[BC_BP:BC_BP + 8, 0:128], dbproj[:].rearrange("(a b) -> a b", b=128), ring="sb")
    dma("sync", attnf[BC_CB1:BC_CB1 + 8, 0:128], dcb1[:].rearrange("(a b) -> a b", b=128), ring="sb")
    dma("sync", attnf[BC_CB2:BC_CB2 + 4, 0:128], dcb2[:].rearrange("(a b) -> a b", b=128), ring="sb")
    dma("sync", attnf[BC_CB3:BC_CB3 + 2, 0:128], dcb3[:].rearrange("(a b) -> a b", b=128), ring="sb")
    dma("sync", attnf[BC_CB4:BC_CB4 + 1, 0:1], dcb4[:].unsqueeze(0), ring="sb")
    dma("sync", attnf[BC_GB1:BC_GB1 + 4, 0:128], dgb1[:].rearrange("(a b) -> a b", b=128), ring="sb")
    dma("sync", attnf[BC_GB2:BC_GB2 + 2, 0:128], dgb2[:].rearrange("(a b) -> a b", b=128), ring="sb")
    dma("sync", attnf[BC_GB3:BC_GB3 + 1, 0:1], dgb3[:].unsqueeze(0), ring="sb")
    t_bld = dma("sync", attnf[BC_AR:BC_AR + 1, 0:1], dar[:].unsqueeze(0), ring="sb")
    t_btr = p.run("tensor",
                  lambda e: e.transpose(PS3[:, 0:64], attnf[0:64, 0:128],
                                        identf[0:64, 0:64]),
                  waits=(t_bld, tok_consts), inc="sp")
    t_bias = act(biasar[:, 0:64], PS3[:, 0:64], AF.Copy, waits=(t_btr,))
    t_bq32 = ts2(biasar[:, 0:8], biasar[:, 0:8], 1.0 / 32.0, ALU.mult, waits=(t_bias,))

    def load_w_section(dram, cols, base, region=None, waits=(), ring="sg"):
        reg = wqA if region is None else region
        tok = None
        ncc = dram.shape[0] // 128
        for cc in range(ncc):
            tok = dma("gpsimd",
                      reg[:, base + cc * cols: base + (cc + 1) * cols],
                      dram[cc * 128:(cc + 1) * 128, :],
                      waits=waits if cc == 0 else (), ring=ring)
        return tok


    def _finish_early():
        for i in range(NTT):
            dma("sync", dy[i * 128:(i + 1) * 128, :], stream[:, 0:1024],
                waits=(p.now("sp"), p.now("sv"), p.now("sa"), p.now("sc")))
        for tok in now_dma("sync") + now_dma("gpsimd"):
            p._wait("sync", tok)
        with nc.Block() as block:
            @block.sync
            def _(e):
                p.emit(e, "sync", handles)
            @block.gpsimd
            def _(e):
                p.emit(e, "gpsimd", handles)
            @block.tensor
            def _(e):
                p.emit(e, "tensor", handles)
            @block.vector
            def _(e):
                p.emit(e, "vector", handles)
            @block.scalar
            def _(e):
                p.emit(e, "scalar", handles)
        st.close()
        return nc, tap_handles

    ev_done = {}
    t_stream = None
    for tt_i in range(NTT):
        slot = temps[:, (tt_i % 2) * 1024:(tt_i % 2) * 1024 + 1024]
        tok_load = dma("sync", slot, dx[tt_i * 128:(tt_i + 1) * 128, :],
                       waits=(ev_done.get(tt_i - 2),), ring="sx0" if tt_i % 2 == 0 else "sx1")
        for half in range(2):
            tr_last = None
            for j in range(4):
                c = half * 4 + j
                if tt_i % 2 == 0:
                    bsl = (PS1 if half == 0 else PS2)[:, j * 128:j * 128 + 128]
                else:
                    bsl = PS0[:, half * 512 + j * 128: half * 512 + j * 128 + 128]
                tr_last = p.run("tensor",
                                lambda e, o=bsl, i=slot[:, c * 128:(c + 1) * 128]:
                                e.transpose(o, i, identf[:]),
                                waits=(tok_load, tok_consts,
                                       ev_done.get((tt_i - 2, half)),
                                       t_btr if tt_i < 2 else None),
                                inc="sp")
            for j in range(4):
                c = half * 4 + j
                if tt_i % 2 == 0:
                    bsl = (PS1 if half == 0 else PS2)[:, j * 128:j * 128 + 128]
                else:
                    bsl = PS0[:, half * 512 + j * 128: half * 512 + j * 128 + 128]
                t_stream = act(chunk(c, tt_i * 128, tt_i * 128 + 128), bsl, AF.Copy,
                               waits=(tr_last, t_bias if tt_i < 2 else None))
            ev_done[(tt_i, half)] = t_stream
        ev_done[tt_i] = t_stream

    if upto == "p1":
        return _finish_early()

    def ln_full(out_mode, guard, pre_cast=None):
        if guard is None:
            gt = ()
        elif isinstance(guard[0], str):
            gt = (guard,)
        else:
            gt = guard
        if pre_cast is not None:
            cast_toks = pre_cast
        else:
            toks = [vcopy(xb(c), chunk(c), waits=gt) for c in range(CH)]
            cast_toks = {(tg, c): toks[c] for tg in range(TGN) for c in range(CH)}
        sqmm_tok = {}
        e1_tok = {}
        var_tok = {}
        out_toks = []
        for tg in range(TGN):
            lo, hi = tg * TGS, (tg + 1) * TGS
            m1 = m2 = None
            for c in range(CH):
                sslot = stgA[:, (c % 4) * 512:(c % 4) * 512 + 512]
                tok_sq = act(sslot, xb(c, lo, hi), AF.Square,
                             waits=(cast_toks[(tg, c)],) + gt +
                                   (sqmm_tok.get((tg, c - 4)), sqmm_tok.get((tg - 1, c + 4))))
                m1 = mm(PS1[:, 0:512], meanpadb[:], xb(c, lo, hi), c == 0, c == CH - 1,
                        waits=(cast_toks[(tg, c)], e1_tok.get(tg - 1)) + gt,
                        inc="sp")
                m2 = mm(PS2[:, 0:512], meanpadb[:], sslot, c == 0, c == CH - 1,
                        waits=(tok_sq, var_tok.get(tg - 1)), inc="sp")
                sqmm_tok[(tg, c)] = m2
            e1 = act(rw(RW_M, TGS), PS1[0:1, 0:512], AF.Copy, waits=(m1,))
            e1_tok[tg] = e1
            v1 = tt(rw(RW_V, TGS), rw(RW_M, TGS), rw(RW_M, TGS), ALU.mult, waits=(e1,))
            v2 = tt(rw(RW_V, TGS), PS2[0:1, 0:512], rw(RW_V, TGS), ALU.subtract,
                    waits=(v1, m2))
            var_tok[tg] = v2
            v3 = act(rw(RW_V, TGS), rw(RW_V, TGS), AF.Ln, bias=EPS, waits=(v2,))
            v4 = act(rw(RW_A, TGS), rw(RW_V, TGS), AF.Exp, scale=-0.5, waits=(v3,))
            v5 = tt(rw(RW_B, TGS), rw(RW_M, TGS), rw(RW_A, TGS), ALU.mult, waits=(v4,))
            prev = tuple(out_toks[-CH:])
            ta = expand(rw(RW_A, TGS), PS3, TGS, waits=(v4,) + prev)
            tb = expand(rw(RW_B, TGS), PS4, TGS, waits=(v5,) + prev)
            for c in range(CH):
                t1 = tt(temps[:, 1536:2048], xb(c, lo, hi), PS3[:, 0:TGS], ALU.mult,
                        waits=(ta, cast_toks[(tg, c)], p.now("sv")))
                dst = xb(c, lo, hi) if out_mode == "xb" else chunk(c, lo, hi)
                out_toks.append(tt(dst, temps[:, 1536:2048], PS4[:, 0:TGS],
                                   ALU.subtract, waits=(tb, t1)))
        return out_toks

    ln1_toks = ln_full("xb", t_stream)
    t_h1 = ln1_toks[-1]
    h1_tg_toks = [ln1_toks[tg * CH + CH - 1] for tg in range(TGN)]

    if "h1" in taps:
        for c in range(CH):
            dma("sync", tap_handles["h1"][c * 128:(c + 1) * 128, :],
                xbA[:].rearrange("p (c t) -> p c t", c=CH)[:, c, :], waits=(t_h1,), ring="st")

    if upto == "ln1":
        return _finish_early()

    def fm_proj_phase(base, bias_col, scale, dst_dram, w_tok, mode, bank_guard,
                      prev_dma=None, h1_tg=None):
        ev = {}
        sq_act = {}
        sq_mm = {}
        dma_toks = {}

        def emit_sq(i):
            et_, tg_ = i // TGN, i % TGN
            sslot = stgA[:, 512 + (i % 2) * 1024: 512 + (i % 2) * 1024 + 512]
            stg_ = stgA[:, (i % 2) * 1024:(i % 2) * 1024 + 512]
            sq_act[i] = act(sslot, stg_, AF.Square,
                            waits=(ev[i], sq_mm.get(i - 2)))
            sq_mm[i] = mm(PS0[:, tg_ * 512: tg_ * 512 + 512], onespadb[:], sslot,
                          et_ == 0, et_ == CH - 1,
                          waits=(sq_act[i],) + ((bank_guard,) if i < TGN else ()),
                          inc="sp")

        idx = 0
        for et in range(CH):
            for tg in range(TGN):
                lo, hi = tg * TGS, (tg + 1) * TGS
                bank = PS1 if idx % 2 == 0 else PS2
                last = None
                for cc in range(CH):
                    w = (w_tok, h1_tg[tg] if h1_tg else t_h1,
                         ev.get(idx - 2), bank_guard if idx == 0 else None)
                    last = mm(bank[:, 0:512],
                              wqA[:, base + cc * D + et * 128: base + cc * D + et * 128 + 128],
                              xb(cc, lo, hi), cc == 0, cc == CH - 1,
                              waits=w if cc == 0 else (), inc="sp" if cc == CH - 1 else None)
                stg = stgA[:, (idx % 2) * 1024:(idx % 2) * 1024 + 512]
                ev[idx] = act(stg, bank[:, 0:512], AF.Identity,
                              bias=bcol(bias_col + et), scale=scale,
                              waits=(last, t_bq32,
                                     dma_toks.get(idx - 2,
                                                  prev_dma[idx % 2] if prev_dma else None),
                                     sq_act.get(idx - 2)))
                dma_toks[idx] = dma("sync", dst_dram[et * 128:(et + 1) * 128, lo:hi],
                                    stg, waits=(ev[idx],),
                                    ring="ss0" if idx % 2 == 0 else "ss1")
                if idx >= 1:
                    emit_sq(idx - 1)
                idx += 1
        emit_sq(idx - 1)
        row_toks = []
        for tg in range(TGN):
            wmm = (sq_mm[(CH - 1) * TGN + tg],)
            if mode == "ksq":
                row_toks.append(p.run(
                    "vector",
                    lambda e, o=rw(RW_MISC + tg, 1), i=PS0[0:1, tg * 512: tg * 512 + 512]:
                    e.reduce_max(o, i, axis=AX.X),
                    waits=wmm, inc="sv"))
            else:
                row_toks.append(act(qsqB[0:1, tg * TGS:(tg + 1) * TGS],
                                    PS0[0:1, tg * 512: tg * 512 + 512], AF.Copy,
                                    waits=wmm))
        return ev[idx - 1], (dma_toks[idx - 2], dma_toks[idx - 1]), row_toks

    t_wk = load_w_section(dwqkv[:, D:2 * D], D, 0, waits=(tok_consts,), ring="w0")
    t_wv = load_w_section(dwqkv[:, 2 * D:3 * D], D, 8192, waits=(tok_consts,), ring="w1")
    k_ev, k_dmas, k_sq = fm_proj_phase(0, BC_BK, 1.0, kag_in, t_wk, "ksq", t_stream,
                                       h1_tg=h1_tg_toks)
    k_dma = k_dmas[1]
    k_sp = p.now("sp")

    t_kmax = p.run("vector",
                   lambda e: e.reduce_max(rw(RW_MISC + 4, 1), rw(RW_MISC, 4), axis=AX.X),
                   waits=tuple(k_sq), inc="sv")
    t_kb = vcopy(rw(RW_MISC + 8, 8), rw(RW_MISC + 4, 1).to_broadcast((1, 8)),
                 waits=(t_kmax,))
    t_kd = dma("gpsimd", kmax_in[:], rw(RW_MISC + 8, 8), waits=(t_kb,), ring="sk")

    v_ev = {}
    v_dma = {}
    idx = 0
    for tt_i in range(NTT):
        for eg in range(2):
            bank = PS1 if idx % 2 == 0 else PS2
            last = None
            for cc in range(CH):
                w = (t_wv, t_h1, v_ev.get(idx - 2), k_ev if idx == 0 else None)
                last = mm(bank[:, 0:512],
                          xb(cc, tt_i * 128, (tt_i + 1) * 128),
                          wqA[:, 8192 + cc * D + eg * 512: 8192 + cc * D + eg * 512 + 512],
                          cc == 0, cc == CH - 1,
                          waits=w if cc == 0 else (), inc="sp" if cc == CH - 1 else None)
            stg = stgA[:, (idx % 2) * 1024:(idx % 2) * 1024 + 512]
            v_ev[idx] = act(stg, bank[:, 0:512], AF.Copy,
                            waits=(last, v_dma.get(idx - 2, k_dmas[idx % 2])))
            v_dma[idx] = dma("sync",
                             vag_in[tt_i * 128:(tt_i + 1) * 128, eg * 512:(eg + 1) * 512],
                             stg, waits=(v_ev[idx],),
                             ring="ss0" if idx % 2 == 0 else "ss1")
            idx += 1
    v_last_ev = v_ev[idx - 1]
    v_last_dma = v_dma[idx - 1]

    if skip_cc:
        t_ar_kmax = dma("gpsimd", kmax_out[:], kmax_in[:], waits=(t_kd,))
    else:
        t_ar_kmax = gp(lambda e: e.collective_compute(
            "AllReduce", ALU.max, replica_groups=[[0, 1, 2, 3], [4, 5, 6, 7]],
            ins=[kmax_in[:]], outs=[kmax_out[:]]),
            waits=(t_kd, k_dmas[0], k_dmas[1]), inc="sc")
    t_kmg = dma("gpsimd", rw(RW_MISC + 16, 8), kmax_out[:], waits=(t_ar_kmax,), ring="sk")

    if skip_cc:
        t_ag_k = dma("gpsimd", kag_out_r(0), kag_in, waits=(k_dma,))
        t_ag_v = dma("gpsimd", vag_out_r(0), vag_in, waits=(v_last_dma,))
    else:
        t_ag_k = gp(lambda e: e.collective_compute(
            "AllGather", ALU.bypass, replica_groups=[[0, 1, 2, 3], [4, 5, 6, 7]],
            ins=[kvag_in_t[:]], outs=[kvag_out_t[:]]),
            waits=(t_kmg, v_dma[idx - 2], v_last_dma), inc="sc")
        t_ag_v = t_ag_k
    if "kag" in taps:
        dma("sync", tap_handles["kag"][:], kag_in, waits=(k_dma, t_ag_k), ring="st")
    if "vag" in taps:
        dma("sync", tap_handles["vag"][:], vag_in, waits=(v_last_dma, t_ag_v), ring="st")

    t_wq = load_w_section(dwqkv[:, 0:D], D, 0, waits=(k_sp,), ring="w2")
    q_ev, q_dmas, q_sq = fm_proj_phase(0, BC_BQ, 1.0 / 32.0, qt_dram, t_wq, "qsq",
                                      v_last_ev, prev_dma=(v_dma[idx - 2], v_dma[idx - 1]))
    q_dma = q_dmas[1]
    if "qt" in taps:
        dma("sync", tap_handles["qt"][:], qt_dram[:], waits=(q_dma,), ring="st")

    t_wproj = load_w_section(dwproj, D, 8192, waits=(p.now("sp"),), ring="w3")
    t_cn1 = load_w_section(dcw1, D, 0, waits=(q_dma, p.now("sp")), ring="w0")

    if upto == "p3":
        return _finish_early()

    xf_sp = None
    for c in range(CH):
        xf_sp = dma("sync", xfm_dram[c * 128:(c + 1) * 128, :], chunk(c),
                    waits=(t_h1,), ring="sxf")

    q_ld = None
    for cc in range(CH):
        q_ld = dma("sync", xbA[:, cc * T:(cc + 1) * T],
                   qt_dram[cc * 128:(cc + 1) * 128, :],
                   waits=(q_dmas[0], q_dmas[1], q_ev), ring="qld")

    def qT(cc, qlo, qhi):
        return xbA[:, cc * T + qlo: cc * T + qhi]

    def emb_sl(qt, lo=0, hi=8 * QTS):
        return stream[:, qt * (CH * QTS) + lo: qt * (CH * QTS) + hi]

    sb1 = tt(rw(0, 1024), qsqB[0:1, 0:1024],
             rw(RW_MISC + 16, 1).to_broadcast((1, 1024)), ALU.mult,
             waits=(t_kmg, q_sq[-1], p.now("sv")))
    sb1b = tt(rw(1024, 1024), qsqB[0:1, 1024:2048],
              rw(RW_MISC + 16, 1).to_broadcast((1, 1024)), ALU.mult, waits=(sb1,))
    sb2 = act(rw(0, 1024), rw(0, 1024), AF.Ln, waits=(sb1,))
    sb2 = act(rw(0, 1024), rw(0, 1024), AF.Exp, scale=0.5, waits=(sb2,))
    sb2b = act(rw(1024, 1024), rw(1024, 1024), AF.Ln, waits=(sb1b,))
    sb2b = act(rw(1024, 1024), rw(1024, 1024), AF.Exp, scale=0.5, waits=(sb2b,))
    sb3 = ts2(rw(0, 1024), rw(0, 1024), -1.01, ALU.mult, waits=(sb2,))
    sb3b = ts2(rw(1024, 1024), rw(1024, 1024), -1.01, ALU.mult, waits=(sb2b,))
    t_pad = vcopy(padb[0:1, 0:1024], rw(0, 1024), waits=(sb3,))
    t_pad = vcopy(padb[0:1, 1024:2048], rw(1024, 1024), waits=(sb3b, t_pad))

    def sums_sl(qt):
        return rows[0:1, qt * QTS:(qt + 1) * QTS]

    use_k = {}
    use_v = {}
    exp_guard = {}
    evac_tok = {}
    sums_tok = {}
    for kb in range(NKB):
        r, ts_off = kb // 4, (kb % 4) * KBS
        par = kb % 2
        kslot = kvA[:, par * 4096: par * 4096 + 4096]
        vslot = kvA[:, 8192:12288]
        tk = dma("sync", kslot.rearrange("p (cc t) -> p cc t", cc=CH),
                 kag_out_r(r).rearrange("(cc p) t -> p cc t", p=128)[:, :, ts_off:ts_off + KBS],
                 waits=(t_ag_v, use_k.get(kb - 2)),
                 ring="kk0" if kb % 2 == 0 else "kk1")
        tv = dma("sync", vslot.rearrange("p (kt d) -> p kt d", kt=4),
                 vag_out_r(r)[ts_off:ts_off + KBS, :].rearrange("(kt p) d -> p kt d", p=128),
                 waits=(t_ag_v, use_v.get(kb - 1)), ring="kv0")
        for qt_i in range(NQT):
            qlo = qt_i * QTS
            exp_toks = []
            stab_tok = {}
            for kt in range(4):
                bank = PS1 if kt < 2 else PS2
                boff = (kt % 2) * 256
                w0 = (tk, q_ld, t_pad, exp_guard.get(kt // 2))
                for cc in range(CH):
                    mm(bank[:, boff:boff + 256],
                       kvA[:, par * 4096 + cc * 512 + kt * 128: par * 4096 + cc * 512 + kt * 128 + 128],
                       qT(cc, qlo, qlo + QTS),
                       cc == 0, False, waits=w0 if cc == 0 else ())
                stab_tok[kt] = mm(bank[:, boff:boff + 256], e0b[:],
                                  padb[:, qlo:qlo + QTS], False, True, inc="sp")
                if kt % 2 == 1:
                    dst = prA[:, (kt // 2) * 512: (kt // 2) * 512 + 512]
                    etok = act(dst, bank[:, 0:512], AF.Exp,
                               waits=(stab_tok[kt - 1], stab_tok[kt]))
                    exp_toks.append(etok)
                    exp_guard[kt // 2] = etok
            if qt_i == NQT - 1:
                use_k[kb] = stab_tok[3]
            av_last = None
            for dt in range(CH):
                for kt in range(4):
                    pr = prA[:, kt * 256: kt * 256 + 256]
                    w = ()
                    if kt == 0:
                        w = (tv, exp_toks[0], evac_tok.get((kb, qt_i - 1) if qt_i > 0 else (kb - 1, NQT - 1)))
                    elif kt == 2:
                        w = (exp_toks[1],)
                    av_last = mm(PS0[:, dt * 256: dt * 256 + 256],
                                 kvA[:, 8192 + kt * 1024 + dt * 128: 8192 + kt * 1024 + dt * 128 + 128],
                                 pr, (kt == 0 and dt % 2 == 0), False,
                                 waits=w, inc="sp" if kt == 3 and dt == CH - 1 else None)
            sums_last = None
            for kt in range(4):
                pr = prA[:, kt * 256: kt * 256 + 256]
                w = ()
                if kt == 0:
                    w = (exp_toks[0], sums_tok.get((kb, qt_i - 1) if qt_i > 0 else (kb - 1, NQT - 1)))
                elif kt == 2:
                    w = (exp_toks[1],)
                sums_last = mm(PS3[:, 0:256], onespadb[:], pr,
                               kt == 0, kt == 3,
                               waits=w, inc="sp" if kt == 3 else None)
            if qt_i == NQT - 1:
                use_v[kb] = av_last
            if kb == 0:
                evac_tok[(kb, qt_i)] = vcopy(emb_sl(qt_i), PS0[:, 0:2048],
                                             waits=(av_last, xf_sp, p.now("sv")))
                sums_tok[(kb, qt_i)] = vcopy(sums_sl(qt_i), PS3[0:1, 0:256],
                                             waits=(sums_last,))
            else:
                evac_tok[(kb, qt_i)] = tt(emb_sl(qt_i), PS0[:, 0:2048], emb_sl(qt_i),
                                          ALU.add, waits=(av_last, p.now("sv")))
                sums_tok[(kb, qt_i)] = tt(sums_sl(qt_i), PS3[0:1, 0:256], sums_sl(qt_i),
                                          ALU.add, waits=(sums_last,))

    x1_last = {}
    t1_done = {}
    k_end = evac_tok[(NKB - 1, NQT - 1)]
    s_end = sums_tok[(NKB - 1, NQT - 1)]
    bgq = {"R0": k_end, "R1": k_end, "A0": k_end, "A1": k_end,
           "B0": s_end, "B1": None, "PS1": None, "PS2": None, "emA": None}
    def xdst(qt, lo=0, hi=8 * QTS):
        if qt == 0:
            return temps[:, lo:hi]
        return emb_sl(qt - 1, lo, hi)
    for qt_i in range(NQT):
        qlo = qt_i * QTS
        xload = None
        for c in range(CH):
            xload = dma("sync", xdst(qt_i, c * QTS, (c + 1) * QTS),
                        xfm_dram[c * 128:(c + 1) * 128, qlo:qlo + QTS],
                        waits=(xf_sp, t1_done.get(qt_i - 1),
                               x1_last.get(qt_i - 2)),
                        ring="xq0" if qt_i % 2 == 0 else "xq1")
        pad_off = (qt_i % 2) * 256
        bcR = PS0[:, (qt_i % 2) * 512: (qt_i % 2) * 512 + QTS]
        bcA = PS0[:, 1024 + (qt_i % 2) * 512: 1024 + (qt_i % 2) * 512 + QTS]
        bcB = PS3[:, 0:QTS] if qt_i % 2 == 0 else PS4[:, 0:QTS]
        wv_l = act(sums_sl(qt_i), sums_sl(qt_i), AF.Ln,
                   waits=(sums_tok[(NKB - 1, qt_i)],))
        wv_r = act(pads[0:1, pad_off:pad_off + QTS], sums_sl(qt_i), AF.Exp,
                   scale=-1.0, waits=(wv_l, last_pad_mm[qt_i % 2]))
        t_bc = mm(bcR, e0f[:], pads[:, pad_off:pad_off + QTS], True, True,
                  waits=(wv_r, bgq.get("R%d" % (qt_i % 2))), inc="sp")
        last_pad_mm[qt_i % 2] = t_bc
        emb_toks = []
        t1_last = None
        for dt in range(CH):
            t1_last = tt(attnf[:, dt * 256:(dt + 1) * 256],
                         emb_sl(qt_i, dt * 256, dt * 256 + 256), bcR,
                         ALU.mult,
                         waits=(t_bc, p.now("sv"),
                                bgq.get("apG") if dt == 0 else None))
            emb_toks.append(act(emA[:, dt * 256:(dt + 1) * 256],
                                attnf[:, dt * 256:(dt + 1) * 256], AF.Identity,
                                bias=bcol(BC_BV + dt),
                                waits=(t1_last, bgq.get("emA"))))
        proj_toks = []
        for et in range(CH):
            bank = PS1 if et % 2 == 0 else PS2
            last = None
            for cc in range(CH):
                w = (t_wproj, emb_toks[-1],
                     proj_toks[et - 2] if et >= 2 else
                     (exp_guard.get(et) if qt_i == 0 else bgq.get("PS1" if et == 0 else "PS2")))
                last = mm(bank[:, 0:256],
                          wqA[:, 8192 + cc * D + et * 128: 8192 + cc * D + et * 128 + 128],
                          emA[:, cc * 256:(cc + 1) * 256],
                          cc == 0, cc == CH - 1,
                          waits=w if cc == 0 else (), inc="sp" if cc == CH - 1 else None)
            proj_toks.append(act(attnf[:, et * 256:(et + 1) * 256], bank[:, 0:256],
                                 AF.Identity, bias=bcol(BC_BP + et),
                                 waits=(last, t1_last if et < 2 else None)))
        t_cast = vcopy(stgA[:, 0:2048].rearrange("p (a b) -> p a b", a=1) if False else stgA[:, 0:2048],
                       attnf[:], waits=(proj_toks[-1], p.now("ss0"), p.now("ss1")))
        t_sq = act(emA[:, 0:2048], attnf[:], AF.Square,
                   waits=(proj_toks[-1], emb_toks[-1]))
        m1 = m2 = None
        for cc in range(CH):
            m1 = mm(PS1[:, 0:256], meanpadb[:], stgA[:, cc * 256:(cc + 1) * 256],
                    cc == 0, cc == CH - 1,
                    waits=(t_cast, proj_toks[-2]) if cc == 0 else (),
                    inc="sp" if cc == CH - 1 else None)
        for cc in range(CH):
            m2 = mm(PS2[:, 0:256], meanpadb[:], emA[:, cc * 256:(cc + 1) * 256],
                    cc == 0, cc == CH - 1,
                    waits=(t_sq, proj_toks[-1]) if cc == 0 else (),
                    inc="sp" if cc == CH - 1 else None)
        e1 = act(rw(RW2_M, QTS), PS1[0:1, 0:256], AF.Copy, waits=(m1,))
        v1 = tt(rw(RW2_V, QTS), rw(RW2_M, QTS), rw(RW2_M, QTS), ALU.mult, waits=(e1,))
        v2 = tt(rw(RW2_V, QTS), PS2[0:1, 0:256], rw(RW2_V, QTS), ALU.subtract,
                waits=(v1, m2))
        v3 = act(rw(RW2_V, QTS), rw(RW2_V, QTS), AF.Ln, bias=EPS, waits=(v2,))
        v4 = act(rw(RW2_A, QTS), rw(RW2_V, QTS), AF.Exp, scale=-0.5, waits=(v3,))
        v5 = tt(rw(RW2_B, QTS), rw(RW2_M, QTS), rw(RW2_A, QTS), ALU.mult, waits=(v4,))
        ta = mm(bcA, e0f[:], pads[:, pad_off:pad_off + QTS], True, True,
                waits=(vcopy(pads[0:1, pad_off:pad_off + QTS], rw(RW2_A, QTS),
                             waits=(v4, last_pad_mm[qt_i % 2])),
                       bgq.get("A%d" % (qt_i % 2))), inc="sp")
        last_pad_mm[qt_i % 2] = ta
        tb = mm(bcB, e0f[:], pads[:, pad_off:pad_off + QTS], True, True,
                waits=(vcopy(pads[0:1, pad_off:pad_off + QTS], rw(RW2_B, QTS),
                             waits=(v5, last_pad_mm[qt_i % 2])),
                       bgq.get("B%d" % (qt_i % 2))), inc="sp")
        last_pad_mm[qt_i % 2] = tb
        ap = None
        u1_latest = None
        for cc in range(CH):
            u1_latest = tt(attnf[:, cc * 256:(cc + 1) * 256],
                           attnf[:, cc * 256:(cc + 1) * 256], bcA,
                           ALU.mult, waits=(ta, t_cast, t_sq, p.now("sv")))
            u2 = tt(attnf[:, cc * 256:(cc + 1) * 256],
                    attnf[:, cc * 256:(cc + 1) * 256], bcB,
                    ALU.subtract, waits=(tb, u1_latest))
            ap = p.run("gpsimd",
                       lambda e, o=xdst(qt_i, cc * QTS, (cc + 1) * QTS),
                       a=attnf[:, cc * 256:(cc + 1) * 256]:
                       e.tensor_tensor(o, o, a, ALU.add),
                       waits=(u2, xload), inc="so")
        x1d = None
        for c in range(CH):
            x1d = dma("sync", x1_dram[c * 128:(c + 1) * 128, qlo:qlo + QTS],
                      xdst(qt_i, c * QTS, (c + 1) * QTS), waits=(ap,),
                      ring="x1q0" if qt_i % 2 == 0 else "x1q1")
        x1_last[qt_i] = x1d
        t1_done[qt_i] = t1_last
        bgq["PS1"] = e1
        bgq["PS2"] = v2
        bgq["emA"] = m2
        bgq["R%d" % (qt_i % 2)] = t1_last
        bgq["A%d" % (qt_i % 2)] = u1_latest
        bgq["B%d" % (qt_i % 2)] = u2
        bgq["apG"] = ap

    t_x1a = t_x1b = None
    for c in range(CH):
        tok = dma("sync", chunk(c), x1_dram[c * 128:(c + 1) * 128, :],
                  waits=(x1_last[NQT - 1], x1_last[NQT - 2],
                         evac_tok[(NKB - 1, 7)], p.now("sv")),
                  ring="sx0" if c % 2 == 0 else "sx1")
    t_x1a = p.now("sx0")
    t_x1b = p.now("sx1")
    t_x1 = (t_x1a, t_x1b)
    x1_spill = x1_last[NQT - 1]

    if "x1" in taps:
        for c in range(CH):
            dma("sync", tap_handles["x1"][c * 128:(c + 1) * 128, :], chunk(c),
                waits=t_x1, ring="st")

    if upto == "attn":
        return _finish_early()

    ln3 = ln_full("xb", t_x1)
    ln3_tg = [ln3[tg * CH + CH - 1] for tg in range(TGN)]
    t_h2 = ln3[-1]
    if "h2" in taps:
        for c in range(CH):
            dma("sync", tap_handles["h2"][c * 128:(c + 1) * 128, :],
                xbA[:].rearrange("p (c t) -> p c t", c=CH)[:, c, :], waits=(t_h2,), ring="st")

    ctx_tok = None
    for c in range(CH):
        ctx_tok = p.run("vector",
                        lambda e, o=ctxP[:, c:c + 1], i=xb(c):
                        e.reduce_sum(o, i, axis=AX.X),
                        waits=(t_h2,), inc="sv")
    t_ctxd = dma("gpsimd", ctx_in[:], ctxP[:, 0:CH], waits=(ctx_tok,), ring="sk")
    if skip_cc:
        t_ar_ctx = dma("gpsimd", ctx_out[:], ctx_in[:], waits=(t_ctxd,))
    else:
        t_ar_ctx = gp(lambda e: e.collective_compute(
            "AllReduce", ALU.add, replica_groups=[[0, 1, 2, 3], [4, 5, 6, 7]],
            ins=[ctx_in[:]], outs=[ctx_out[:]]),
        waits=(t_ctxd, x1_spill) + now_dma("sync") + now_dma("gpsimd"), inc="sc")
    t_ctxl = dma("gpsimd", ctxP[:, 8:16], ctx_out[:], waits=(t_ar_ctx,), ring="sk")
    t_ctxs = ts2(ctxP[:, 8:16], ctxP[:, 8:16], 1.0 / S, ALU.mult, waits=(t_ctxl,))
    if "ctx" in taps:
        dma("sync", tap_handles["ctx"][:], ctxP[:, 8:16], waits=(t_ctxs,), ring="st")
    ctxb = stgA[:, 0:8]
    t_ctxb = vcopy(ctxb, ctxP[:, 8:16], waits=(t_ctxs, p.now("sp")))

    if upto == "ln3":
        return _finish_early()

    t_cn2 = load_w_section(dcw2, 512, 8192, waits=(p.now("sp"),), ring="w1")
    t_cn3 = load_w_section(dcw3, 256, 12288, waits=(t_cn2,), ring="w2")
    mz1 = gp(lambda e: e.memset(wqA[:, 13312:13568], 0.0), waits=(t_cn3,),
             inc="so", inck=1)
    t_cn4a = dma("gpsimd", wqA[:, 13312:13313], dcw4[0:128, :], waits=(mz1,), ring="w3")
    t_cn4 = dma("gpsimd", wqA[:, 13440:13441], dcw4[128:256, :], ring="w3")
    t_gc2 = load_w_section(dgw2, 256, 13568, waits=(t_cn4,), ring="w0")
    mz2 = gp(lambda e: e.memset(wqA[:, 14592:14848], 0.0), waits=(t_gc2,),
             inc="so", inck=1)
    t_gc3a = dma("gpsimd", wqA[:, 14592:14593], dgw3[0:128, :], waits=(mz2,), ring="w1")
    t_gc3 = dma("gpsimd", wqA[:, 14720:14721], dgw3[128:256, :], ring="w1")
    t_gc1 = load_w_section(dgw1, 512, 7168, region=kvA, waits=(p.now("sp"),), ring="w2")
    t_wffn = (p.now("w0"), p.now("w1"), p.now("w2"), p.now("w3"))

    gc_toks = {}
    def emit_gc():
        gc1_toks = []
        for et in range(4):
            last = None
            for cc in range(CH):
                last = mm(PS0[:, et * 512:et * 512 + 1],
                          kvA[:, 7168 + cc * 512 + et * 128: 7168 + cc * 512 + et * 128 + 128],
                          ctxb[:, cc:cc + 1], cc == 0, cc == CH - 1,
                          waits=t_wffn + (t_ctxb,) if cc == 0 else (),
                          inc="sp" if cc == CH - 1 else None)
            gc1_toks.append(act(emA[:, 1024 + et:1024 + et + 1], PS0[:, et * 512:et * 512 + 1],
                                AF.Gelu, bias=bcol(BC_GB1 + et), waits=(last,)))
        gc2_toks = []
        for et in range(2):
            last = None
            for cc in range(4):
                last = mm(PS0[:, et * 512 + 256:et * 512 + 257],
                          wqA[:, 13568 + cc * 256 + et * 128: 13568 + cc * 256 + et * 128 + 128],
                          emA[:, 1024 + cc:1024 + cc + 1], cc == 0, cc == 3,
                          waits=t_wffn + (gc1_toks[-1],) if cc == 0 else (),
                          inc="sp" if cc == 3 else None)
            gc2_toks.append(act(emA[:, 1032 + et:1032 + et + 1], PS0[:, et * 512 + 256:et * 512 + 257],
                                AF.Gelu, bias=bcol(BC_GB2 + et), waits=(last,)))
        last = None
        for cc in range(2):
            last = mm(PS0[:, 1024:1025], wqA[:, 14592 + cc * 128:14592 + cc * 128 + 128],
                      emA[:, 1032 + cc:1032 + cc + 1], cc == 0, cc == 1,
                      waits=t_wffn + (gc2_toks[-1],) if cc == 0 else (),
                      inc="sp" if cc == 1 else None)
        t_gf = act(bcol(BC_GF, 0, 1), PS0[0:1, 1024:1025], AF.Sigmoid,
                   bias=bcol(BC_GB3, 0, 1), waits=(last,))
        s1 = ts2(bcol(BC_T1, 0, 1), bcol(BC_GF, 0, 1), 1.4, ALU.mult, waits=(t_gf,))
        s2 = tt(bcol(BC_T1, 0, 1), bcol(BC_T1, 0, 1), bcol(BC_AR, 0, 1), ALU.mult,
                waits=(s1, t_bias))
        s3 = tt(bcol(BC_T1, 0, 1), bcol(BC_T1, 0, 1), bcol(BC_AR, 0, 1), ALU.subtract,
                waits=(s2,))
        s4 = ts2(bcol(BC_SADD, 0, 1), bcol(BC_T1, 0, 1), 0.1, ALU.add, waits=(s3,))
        t_smul = ts2(bcol(BC_SMUL, 0, 1), bcol(BC_AR, 0, 1), 0.6, ALU.mult, waits=(t_bias,))

        gc_toks['s4'] = s4
        gc_toks['smul'] = t_smul

    cyc_done = t_h2
    ln4_casts = {}
    for cyc in range(3):
        if cyc > 0:
            src_tok_pair = (p.now("cs0"), p.now("cs1"))
        else:
            src_tok_pair = ()
        hp = {}
        apply_done = {}
        evt_toks = {}

        def emit_apply(tg, cyc=cyc, hp=hp, apply_done=apply_done, evt_toks=evt_toks,
                       src_tok_pair=src_tok_pair):
            lo, hi = tg * TGS, (tg + 1) * TGS
            c1 = ts2(rw(RW_STG, TGS), tfB[0:1, lo:hi], bcol(BC_SMUL, 0, 1), ALU.mult,
                     bcol(BC_SADD, 0, 1), ALU.add,
                     waits=(gc_toks['s4'], gc_toks['smul'], evt_toks[tg]))
            prev = tuple(apply_done.get((tg - 1, c)) for c in range(CH))
            tc = expand(rw(RW_STG, TGS), PS3, TGS, waits=(c1,) + prev)
            c2 = ts2(rw(RW_STG, TGS), rw(RW_STG, TGS), -1.0, ALU.mult, waits=(tc,))
            tn = expand(rw(RW_STG, TGS), PS4, TGS, waits=(c2,) + prev)
            last = None
            for c in range(CH):
                src_c, sign = TWIST[c]
                bcp, bct = (PS3, tc) if sign > 0 else (PS4, tn)
                u1 = tt(temps[:, 1536:2048], xb(src_c, lo, hi), bcp[:, 0:TGS],
                        ALU.mult, waits=(bct, p.now("sv")) + src_tok_pair)
                base = xb(c, lo, hi) if cyc == 0 else chunk(c, lo, hi)
                last = tt(chunk(c, lo, hi), base, temps[:, 1536:2048], ALU.add,
                          waits=(u1, x1_spill if cyc == 0 else None))
                apply_done[(tg, c)] = last
            return last

        for tg in range(TGN):
            lo, hi = tg * TGS, (tg + 1) * TGS
            h1_toks = []
            for et in range(CH):
                bank = PS1 if et % 2 == 0 else PS2
                last = None
                for cc in range(CH):
                    w = t_wffn + src_tok_pair + (
                         ln3_tg[tg] if cyc == 0 else None,
                         hp.get(("h1", tg, et - 2)),
                         hp.get(("h4", tg - 1)) if et < 2 else None)
                    last = mm(bank[:, 0:512],
                              wqA[:, cc * D + et * 128: cc * D + et * 128 + 128],
                              xb(cc, lo, hi), cc == 0, cc == CH - 1,
                              waits=w if cc == 0 else (),
                              inc="sp" if cc == CH - 1 else None)
                ev = act(kvA[:, et * 512:(et + 1) * 512], bank[:, 0:512], AF.Gelu,
                         bias=bcol(BC_CB1 + et), waits=(last,))
                hp[("h1", tg, et)] = ev
                h1_toks.append(ev)
            if cyc == 0 and tg == 0:
                emit_gc()
            h2_toks = []
            for et in range(4):
                bank = PS1 if et % 2 == 0 else PS2
                last = None
                for cc in range(CH):
                    w = t_wffn + (h1_toks[-1], hp.get(("h2", tg, et - 2)))
                    last = mm(bank[:, 0:512],
                              wqA[:, 8192 + cc * 512 + et * 128: 8192 + cc * 512 + et * 128 + 128],
                              kvA[:, cc * 512:(cc + 1) * 512], cc == 0, cc == CH - 1,
                              waits=w if cc == 0 else (),
                              inc="sp" if cc == CH - 1 else None)
                ev = act(kvA[:, 4096 + et * 512: 4096 + (et + 1) * 512], bank[:, 0:512],
                         AF.Gelu, bias=bcol(BC_CB2 + et), waits=(last,))
                hp[("h2", tg, et)] = ev
                h2_toks.append(ev)
            h3_toks = []
            for et in range(2):
                bank = PS1 if et % 2 == 0 else PS2
                last = None
                for cc in range(4):
                    w = t_wffn + (h2_toks[-1], hp.get(("h3", tg, et - 2)))
                    last = mm(bank[:, 0:512],
                              wqA[:, 12288 + cc * 256 + et * 128: 12288 + cc * 256 + et * 128 + 128],
                              kvA[:, 4096 + cc * 512: 4096 + (cc + 1) * 512],
                              cc == 0, cc == 3,
                              waits=w if cc == 0 else (), inc="sp" if cc == 3 else None)
                ev = act(kvA[:, 6144 + et * 512: 6144 + (et + 1) * 512], bank[:, 0:512],
                         AF.Gelu, bias=bcol(BC_CB3 + et), waits=(last,))
                hp[("h3", tg, et)] = ev
                h3_toks.append(ev)
            last = None
            for cc in range(2):
                w = t_wffn + (h3_toks[0], h3_toks[1], hp.get(("h4", tg - 1)))
                last = mm(PS1[:, 0:512],
                          wqA[:, 13312 + cc * 128: 13312 + cc * 128 + 128],
                          kvA[:, 6144 + cc * 512: 6144 + (cc + 1) * 512],
                          cc == 0, cc == 1,
                          waits=w if cc == 0 else (), inc="sp" if cc == 1 else None)
            evt = act(tfB[0:1, lo:hi], PS1[0:1, 0:512], AF.Sigmoid,
                      bias=bcol(BC_CB4, 0, 1), waits=(last,))
            hp[("h4", tg)] = evt
            evt_toks[tg] = evt
            if tg > 0:
                ap_tok = emit_apply(tg - 1)
                lo1, hi1 = (tg - 1) * TGS, tg * TGS
                if cyc == 2:
                    for c in range(CH):
                        ln4_casts[(tg - 1, c)] = vcopy(
                            xb(c, lo1, hi1), chunk(c, lo1, hi1), waits=(ap_tok,))
                else:
                    for c in range(CH):
                        dma("gpsimd", xb(c, lo1, hi1), chunk(c, lo1, hi1),
                            waits=(ap_tok,), ring="cs0" if c % 2 == 0 else "cs1")
        cyc_done = emit_apply(TGN - 1)
        lo1, hi1 = (TGN - 1) * TGS, TGN * TGS
        if cyc == 2:
            for c in range(CH):
                ln4_casts[(TGN - 1, c)] = vcopy(
                    xb(c, lo1, hi1), chunk(c, lo1, hi1), waits=(cyc_done,))
        else:
            for c in range(CH):
                dma("gpsimd", xb(c, lo1, hi1), chunk(c, lo1, hi1),
                    waits=(cyc_done,), ring="cs0" if c % 2 == 0 else "cs1")

    if "ffn" in taps:
        for c in range(CH):
            dma("sync", tap_handles["ffn"][c * 128:(c + 1) * 128, :], chunk(c),
                waits=(cyc_done,), ring="st")

    if upto == "ffn":
        return _finish_early()

    ln4 = ln_full("stream", cyc_done, pre_cast=ln4_casts)
    t_z = ln4[-1]
    add_toks = []
    tg_done = {}
    y_prev = {}
    ydma_prev = {}
    ti = 0
    for tg in range(TGN):
        lo, hi = tg * TGS, (tg + 1) * TGS
        for c in range(CH):
            slot = attnf[:, (c % 4) * 512:(c % 4) * 512 + 512]
            tl = dma("sync", slot, x1_dram[c * 128:(c + 1) * 128, lo:hi],
                     waits=(x1_spill, t_z,
                            add_toks[-4] if len(add_toks) >= 4 else None),
                     ring="xr%d" % (c % 4))
            add_toks.append(tt(chunk(c, lo, hi), chunk(c, lo, hi), slot, ALU.add,
                               waits=(tl, t_z)))
        tg_done[tg] = add_toks[-1]
        for tt_j in range(4):
            tt_i = tg * 4 + tt_j
            ystg = temps[:, (tt_i % 2) * 1024:(tt_i % 2) * 1024 + 1024]
            evs = []
            for half in range(2):
                tr_last = None
                for j in range(4):
                    c = half * 4 + j
                    if tt_i % 2 == 0:
                        bsl = (PS1 if half == 0 else PS2)[:, j * 128:j * 128 + 128]
                    else:
                        bsl = PS0[:, half * 512 + j * 128: half * 512 + j * 128 + 128]
                    tr_last = p.run("tensor",
                                    lambda e, o=bsl, i=chunk(c, tt_i * 128, (tt_i + 1) * 128):
                                    e.transpose(o, i, identf[:]),
                                    waits=(tg_done[tg], y_prev.get((tt_i - 2, half))),
                                    inc="sp")
                for j in range(4):
                    c = half * 4 + j
                    if tt_i % 2 == 0:
                        bsl = (PS1 if half == 0 else PS2)[:, j * 128:j * 128 + 128]
                    else:
                        bsl = PS0[:, half * 512 + j * 128: half * 512 + j * 128 + 128]
                    ev = act(ystg[:, c * 128:(c + 1) * 128], bsl, AF.Copy,
                             waits=(tr_last, ydma_prev.get(tt_i - 2)))
                    evs.append(ev)
                y_prev[(tt_i, half)] = evs[-1]
            ydma_prev[tt_i] = dma("sync", dy[tt_i * 128:(tt_i + 1) * 128, :], ystg,
                                  waits=tuple(evs[-8:]),
                                  ring="sy0" if tt_i % 2 == 0 else "sy1")
            ti += 1

    for tok in now_dma("sync") + now_dma("gpsimd"):
        p._wait("sync", tok)

    with nc.Block() as block:
        @block.sync
        def _(e):
            p.emit(e, "sync", handles)

        @block.gpsimd
        def _(e):
            p.emit(e, "gpsimd", handles)

        @block.tensor
        def _(e):
            p.emit(e, "tensor", handles)

        @block.vector
        def _(e):
            p.emit(e, "vector", handles)

        @block.scalar
        def _(e):
            p.emit(e, "scalar", handles)

    st.close()
    return nc, tap_handles


_CACHE = {}

W_KEYS = {
    "w_qkv": "L0_attn_w_qkv", "b_qkv": "L0_attn_b_qkv",
    "w_proj": "L0_attn_w_proj", "b_proj": "L0_attn_b_proj",
    "cn_w1": "L0_ffn_cn_w1", "cn_b1": "L0_ffn_cn_b1",
    "cn_w2": "L0_ffn_cn_w2", "cn_b2": "L0_ffn_cn_b2",
    "cn_w3": "L0_ffn_cn_w3", "cn_b3": "L0_ffn_cn_b3",
    "cn_w4": "L0_ffn_cn_w4", "cn_b4": "L0_ffn_cn_b4",
    "gc_w1": "L0_ffn_gc_w1", "gc_b1": "L0_ffn_gc_b1",
    "gc_w2": "L0_ffn_gc_w2", "gc_b2": "L0_ffn_gc_b2",
    "gc_w3": "L0_ffn_gc_w3", "gc_b3": "L0_ffn_gc_b3",
    "adaptive_range": "L0_ffn_adaptive_range",
}


def kernel(**inputs):
    x = np.ascontiguousarray(np.asarray(inputs["L0_attn_x"], dtype=np.float32))
    weights = {k: np.ascontiguousarray(np.asarray(inputs[v], dtype=np.float32))
               for k, v in W_KEYS.items()}
    if "nc" not in _CACHE:
        _CACHE["nc"] = build()[0]
    nc = _CACHE["nc"]
    in_maps = []
    for core in range(8):
        b, g = core // 4, core % 4
        m = dict(weights)
        m["x"] = np.ascontiguousarray(x[b, g * T:(g + 1) * T, :])
        in_maps.append(m)
    res = run_bass_kernel_spmd(nc, in_maps, core_ids=list(range(8)))
    y = np.empty((2, S, D), dtype=np.float32)
    for core in range(8):
        b, g = core // 4, core % 4
        y[b, g * T:(g + 1) * T, :] = res.results[core]["y"]
    return y


# version 10
# speedup vs baseline: 1.0024x; 1.0024x over previous
import numpy as np
import ml_dtypes
from contextlib import ExitStack

import concourse.bass as bass
import concourse.mybir as mybir
from concourse.bass_utils import run_bass_kernel_spmd

F32 = mybir.dt.float32
BF16 = mybir.dt.bfloat16
AF = mybir.ActivationFunctionType
ALU = mybir.AluOpType
AX = mybir.AxisListType

T = 2048
D = 1024
S = 8192
CH = 8
NTT = 16
TGN, TGS = 4, 512
NQT, QTS = 8, 256
NKB, KBS = 16, 512
EPS = 1e-5

ENGINES = ("sync", "gpsimd", "tensor", "vector", "scalar")
CC_BARRIER = True

RW_M, RW_V, RW_A, RW_B = 0, 512, 1024, 1536
RW_MISC = 2048
RW_STG = 2112
RW2_M, RW2_V, RW2_A, RW2_B = 2112, 2368, 2624, 2880
ROWS_COLS = 3136

BC_BQ, BC_BK, BC_BV, BC_BP = 0, 8, 16, 24
BC_CB1, BC_CB2, BC_CB3, BC_CB4 = 32, 40, 44, 46
BC_GB1, BC_GB2, BC_GB3, BC_AR = 47, 51, 53, 54
BC_T1, BC_SADD, BC_SMUL, BC_GF = 55, 56, 57, 58

TWIST = {0: (4, +1), 1: (5, +1), 2: (6, -1), 3: (7, -1),
         4: (0, -1), 5: (1, -1), 6: (2, +1), 7: (3, +1)}


class Prog:

    def __init__(self):
        self.ops = {e: [] for e in ENGINES}
        self.cnt = {}
        self.waited = {e: {} for e in ENGINES}

    def _wait(self, eng, tok):
        if tok is None:
            return
        s, v = tok
        if v <= 0 or self.waited[eng].get(s, 0) >= v:
            return
        self.waited[eng][s] = v
        self.ops[eng].append(("w", s, v))

    def run(self, eng, build, waits=(), inc=None, inck=1):
        for tok in waits:
            self._wait(eng, tok)
        if inc is not None:
            self.cnt[inc] = self.cnt.get(inc, 0) + inck
            tok = (inc, self.cnt[inc])
            self.ops[eng].append(("i", build, inc, inck))
            return tok
        self.ops[eng].append(("r", build))
        return None

    def now(self, sem):
        return (sem, self.cnt.get(sem, 0))

    def emit(self, eng_obj, eng_name, handles):
        for op in self.ops[eng_name]:
            if op[0] == "w":
                eng_obj.wait_ge(handles[op[1]], op[2])
            elif op[0] == "i":
                op[1](eng_obj).then_inc(handles[op[2]], op[3])
            else:
                op[1](eng_obj)


def build(taps=(), upto=None, skip_cc=False):
    nc = bass.Bass("TRN2", target_bir_lowering=False)
    taps = set(taps)

    dx = nc.dram_tensor("x", [T, D], F32, kind="ExternalInput")
    dwqkv = nc.dram_tensor("w_qkv", [D, 3 * D], F32, kind="ExternalInput")
    dbqkv = nc.dram_tensor("b_qkv", [3 * D], F32, kind="ExternalInput")
    dwproj = nc.dram_tensor("w_proj", [D, D], F32, kind="ExternalInput")
    dbproj = nc.dram_tensor("b_proj", [D], F32, kind="ExternalInput")
    dcw1 = nc.dram_tensor("cn_w1", [D, D], F32, kind="ExternalInput")
    dcb1 = nc.dram_tensor("cn_b1", [D], F32, kind="ExternalInput")
    dcw2 = nc.dram_tensor("cn_w2", [D, 512], F32, kind="ExternalInput")
    dcb2 = nc.dram_tensor("cn_b2", [512], F32, kind="ExternalInput")
    dcw3 = nc.dram_tensor("cn_w3", [512, 256], F32, kind="ExternalInput")
    dcb3 = nc.dram_tensor("cn_b3", [256], F32, kind="ExternalInput")
    dcw4 = nc.dram_tensor("cn_w4", [256, 1], F32, kind="ExternalInput")
    dcb4 = nc.dram_tensor("cn_b4", [1], F32, kind="ExternalInput")
    dgw1 = nc.dram_tensor("gc_w1", [D, 512], F32, kind="ExternalInput")
    dgb1 = nc.dram_tensor("gc_b1", [512], F32, kind="ExternalInput")
    dgw2 = nc.dram_tensor("gc_w2", [512, 256], F32, kind="ExternalInput")
    dgb2 = nc.dram_tensor("gc_b2", [256], F32, kind="ExternalInput")
    dgw3 = nc.dram_tensor("gc_w3", [256, 1], F32, kind="ExternalInput")
    dgb3 = nc.dram_tensor("gc_b3", [1], F32, kind="ExternalInput")
    dar = nc.dram_tensor("adaptive_range", [1], F32, kind="ExternalInput")
    dy = nc.dram_tensor("y", [T, D], F32, kind="ExternalOutput")

    qt_dram = nc.dram_tensor("qt_dram", [D, T], BF16)
    kvag_in_t = nc.dram_tensor("kvag_in", [2 * D * T], BF16)
    kvag_out_t = nc.dram_tensor("kvag_out", [4 * 2 * D * T], BF16)
    kag_in = kvag_in_t[0: D * T].rearrange("(d t) -> d t", t=T)
    vag_in = kvag_in_t[D * T: 2 * D * T].rearrange("(t d) -> t d", d=D)
    def kag_out_r(r):
        return kvag_out_t[r * 2 * D * T: r * 2 * D * T + D * T].rearrange(
            "(d t) -> d t", t=T)
    def vag_out_r(r):
        return kvag_out_t[r * 2 * D * T + D * T: (r + 1) * 2 * D * T].rearrange(
            "(t d) -> t d", d=D)
    kmax_in = nc.dram_tensor("kmax_in", [1, 8], F32)
    kmax_out = nc.dram_tensor("kmax_out", [1, 8], F32)
    ctx_in = nc.dram_tensor("ctx_in", [128, CH], F32)
    ctx_out = nc.dram_tensor("ctx_out", [128, CH], F32)
    x1_dram = nc.dram_tensor("x1_dram", [D, T], F32)
    xfm_dram = nc.dram_tensor("xfm_dram", [D, T], F32)

    tap_handles = {}
    def add_tap(name, shape, dtype):
        if name in taps:
            tap_handles[name] = nc.dram_tensor("tap_" + name, shape, dtype,
                                               kind="ExternalOutput")
    add_tap("h1", [D, T], BF16)
    add_tap("kag", [D, T], BF16)
    add_tap("vag", [T, D], BF16)
    add_tap("qt", [D, T], BF16)
    add_tap("x1", [D, T], F32)
    add_tap("h2", [D, T], BF16)
    add_tap("ffn", [D, T], F32)
    add_tap("ctx", [128, CH], F32)
    add_tap("x1d", [D, T], F32)
    add_tap("yfm", [D, T], F32)

    st = ExitStack()
    def sbt(name, shape, dtype):
        return st.enter_context(nc.sbuf_tensor(name, shape, dtype))
    def pst(name, shape):
        return st.enter_context(nc.psum_tensor(name, shape, F32))

    stream = sbt("stream", [128, CH * T], F32)
    temps = sbt("temps", [128, 2048], F32)
    rows = sbt("rows", [128, ROWS_COLS], F32)
    pads = sbt("pads", [128, 512], F32)
    attnf = sbt("attnf", [128, 2048], F32)
    biasar = sbt("biasar", [128, 64], F32)
    e0f = sbt("e0f", [128, 128], F32)
    identf = sbt("identf", [128, 128], F32)
    ctxP = sbt("ctxP", [128, 16], F32)
    cst = sbt("cst", [128, 2], F32)

    xbA = sbt("xbA", [128, CH * T], BF16)
    wqA = sbt("wqA", [128, 16384], BF16)
    kvA = sbt("kvA", [128, 12288], BF16)
    prA = sbt("prA", [128, 1024], BF16)
    emA = sbt("emA", [128, 2048], BF16)
    stgA = sbt("stgA", [128, 2048], BF16)
    padb = sbt("padb", [128, 2048], BF16)
    e0b = sbt("e0b", [128, 128], BF16)
    onespadb = sbt("onespadb", [128, 128], BF16)
    meanpadb = sbt("meanpadb", [128, 128], BF16)
    qsqB = sbt("qsqB", [128, 2048], BF16)
    tfB = sbt("tfB", [128, 2048], BF16)

    PS0 = pst("PS0", [128, 2048])
    PS1 = pst("PS1", [128, 512])
    PS2 = pst("PS2", [128, 512])
    PS3 = pst("PS3", [128, 512])
    PS4 = pst("PS4", [128, 512])

    sems = ["sd", "sg", "so", "sp", "sv", "sa", "sc", "sb", "sk", "st",
            "sx0", "sx1", "ss0", "ss1", "kk0", "kk1", "kv0", "sq0", "sxsp",
            "xr0", "xr1", "xr2", "xr3", "sy0", "sy1", "w0", "w1", "w2", "w3",
            "xq0", "xq1", "x1q0", "x1q1", "sxf", "qld", "cs0", "cs1"]
    SYNC_RINGS = ("sd", "sb", "st", "sx0", "sx1", "ss0", "ss1", "kk0", "kk1",
                  "kv0", "sxsp", "xr0", "xr1", "xr2", "xr3", "sy0", "sy1", "xq0", "xq1", "x1q0", "x1q1", "sxf", "qld", "cs0", "cs1")
    GP_RINGS = ("sg", "sk", "sq0", "w0", "w1", "w2", "w3")
    handles = {s: st.enter_context(nc.semaphore(s)) for s in sems}

    nc.const_aps.register(F32, 0.0, cst[:, 0:1])
    nc.const_aps.register(F32, EPS, cst[:, 1:2])

    p = Prog()

    def chunk(c, lo=0, hi=T):
        return stream[:, c * T + lo: c * T + hi]

    def xb(c, lo=0, hi=T):
        return xbA[:, c * T + lo: c * T + hi]

    def rw(col, n):
        return rows[0:1, col:col + n]

    def bcol(c0, p0=0, p1=128):
        return biasar[p0:p1, c0:c0 + 1]

    def dma(eng, out, in_, waits=(), ring=None):
        sem = ring if ring is not None else ("sd" if eng == "sync" else "sg")
        return p.run(eng, lambda e, o=out, i=in_: e.dma_start(out=o, in_=i),
                     waits=waits, inc=sem, inck=16)

    def now_dma(eng):
        rings = SYNC_RINGS if eng == "sync" else GP_RINGS
        return tuple(p.now(s) for s in rings)

    def mm(out, lhsT, rhs, start, stop, waits=(), inc=None):
        return p.run("tensor",
                     lambda e, o=out, l=lhsT, r=rhs, s1=start, s2=stop:
                     e.matmul(o, l, r, start=s1, stop=s2, skip_group_check=True),
                     waits=waits, inc=inc)

    def act(out, in_, func, bias=0.0, scale=1.0, waits=(), inc="sa"):
        return p.run("scalar",
                     lambda e, o=out, i=in_, f=func, b=bias, s=scale:
                     e.activation(out=o, in_=i, func=f, bias=b, scale=s),
                     waits=waits, inc=inc)

    def tt(out, a, b, op, waits=(), inc="sv"):
        return p.run("vector",
                     lambda e, o=out, x=a, y=b, z=op: e.tensor_tensor(o, x, y, z),
                     waits=waits, inc=inc)

    def ts2(out, in_, s1, op0, s2=None, op1=None, waits=(), inc="sv"):
        if op1 is None:
            fn = lambda e, o=out, i=in_, a=s1, p0=op0: \
                e.tensor_scalar(out=o, in0=i, scalar1=a, scalar2=None, op0=p0)
        else:
            fn = lambda e, o=out, i=in_, a=s1, b=s2, p0=op0, p1=op1: \
                e.tensor_scalar(out=o, in0=i, scalar1=a, scalar2=b, op0=p0, op1=p1)
        return p.run("vector", fn, waits=waits, inc=inc)

    def vcopy(out, in_, waits=(), inc="sv"):
        return p.run("vector", lambda e, o=out, i=in_: e.tensor_copy(o, i),
                     waits=waits, inc=inc)

    def vrecip(out, in_, waits=()):
        return p.run("vector", lambda e, o=out, i=in_: e.reciprocal(o, i),
                     waits=waits, inc="sv")

    def gp(build_fn, waits=(), inc=None, inck=1):
        return p.run("gpsimd", build_fn, waits=waits, inc=inc, inck=inck)

    last_pad_mm = [None, None]
    def expand(row_ap, pb, n, waits=()):
        wv = vcopy(pads[0:1, 0:n], row_ap,
                   waits=tuple(waits) + (last_pad_mm[0], last_pad_mm[1]))
        tok = mm(pb[:, 0:n], e0f[:], pads[:, 0:n], True, True,
                 waits=(wv,), inc="sp")
        last_pad_mm[0] = tok
        last_pad_mm[1] = tok
        return tok

    gp(lambda e: e.memset(cst[:, 0:1], 0.0))
    gp(lambda e: e.memset(cst[:, 1:2], EPS))
    gp(lambda e: e.memset(pads[:], 0.0))
    gp(lambda e: e.memset(padb[:], 0.0))
    m1_ = gp(lambda e: e.memset(e0f[:], 0.0), inc="so", inck=1)
    m2_ = gp(lambda e: e.memset(e0b[:], 0.0), inc="so", inck=1)
    m3_ = gp(lambda e: e.memset(onespadb[:], 0.0), inc="so", inck=1)
    m4_ = gp(lambda e: e.memset(meanpadb[:], 0.0), inc="so", inck=1)
    m5_ = gp(lambda e: e.memset(identf[:], 0.0), inc="so", inck=1)
    gp(lambda e: e.affine_select(out=e0f[:], in_=e0f[:], compare_op=ALU.not_equal,
                                 fill=1.0, base=0, pattern=[[0, 128]], channel_multiplier=1),
       waits=(m5_,))
    gp(lambda e: e.affine_select(out=e0b[:], in_=e0b[:], compare_op=ALU.not_equal,
                                 fill=1.0, base=0, pattern=[[0, 128]], channel_multiplier=1),
       waits=(m2_,))
    gp(lambda e: e.affine_select(out=onespadb[:], in_=onespadb[:], compare_op=ALU.not_equal,
                                 fill=1.0, base=0, pattern=[[1, 128]], channel_multiplier=0),
       waits=(m3_,))
    gp(lambda e: e.affine_select(out=meanpadb[:], in_=meanpadb[:], compare_op=ALU.not_equal,
                                 fill=1.0 / D, base=0, pattern=[[1, 128]], channel_multiplier=0),
       waits=(m4_,))
    tok_consts = gp(lambda e: e.affine_select(out=identf[:], in_=identf[:],
                                              compare_op=ALU.not_equal,
                                              fill=1.0, base=0, pattern=[[-1, 128]],
                                              channel_multiplier=1),
                    waits=(m1_,), inc="so", inck=1)

    m6_ = gp(lambda e: e.memset(attnf[:, 0:128], 0.0), inc="so", inck=1)
    dma("sync", attnf[0:24, 0:128], dbqkv[:].rearrange("(a b) -> a b", b=128),
        waits=(m6_,), ring="sb")
    dma("sync", attnf[BC_BP:BC_BP + 8, 0:128], dbproj[:].rearrange("(a b) -> a b", b=128), ring="sb")
    dma("sync", attnf[BC_CB1:BC_CB1 + 8, 0:128], dcb1[:].rearrange("(a b) -> a b", b=128), ring="sb")
    dma("sync", attnf[BC_CB2:BC_CB2 + 4, 0:128], dcb2[:].rearrange("(a b) -> a b", b=128), ring="sb")
    dma("sync", attnf[BC_CB3:BC_CB3 + 2, 0:128], dcb3[:].rearrange("(a b) -> a b", b=128), ring="sb")
    dma("sync", attnf[BC_CB4:BC_CB4 + 1, 0:1], dcb4[:].unsqueeze(0), ring="sb")
    dma("sync", attnf[BC_GB1:BC_GB1 + 4, 0:128], dgb1[:].rearrange("(a b) -> a b", b=128), ring="sb")
    dma("sync", attnf[BC_GB2:BC_GB2 + 2, 0:128], dgb2[:].rearrange("(a b) -> a b", b=128), ring="sb")
    dma("sync", attnf[BC_GB3:BC_GB3 + 1, 0:1], dgb3[:].unsqueeze(0), ring="sb")
    t_bld = dma("sync", attnf[BC_AR:BC_AR + 1, 0:1], dar[:].unsqueeze(0), ring="sb")
    t_btr = p.run("tensor",
                  lambda e: e.transpose(PS3[:, 0:64], attnf[0:64, 0:128],
                                        identf[0:64, 0:64]),
                  waits=(t_bld, tok_consts), inc="sp")
    t_bias = act(biasar[:, 0:64], PS3[:, 0:64], AF.Copy, waits=(t_btr,))
    t_bq32 = ts2(biasar[:, 0:8], biasar[:, 0:8], 1.0 / 32.0, ALU.mult, waits=(t_bias,))

    def load_w_section(dram, cols, base, region=None, waits=(), ring="sg"):
        reg = wqA if region is None else region
        tok = None
        ncc = dram.shape[0] // 128
        for cc in range(ncc):
            tok = dma("gpsimd",
                      reg[:, base + cc * cols: base + (cc + 1) * cols],
                      dram[cc * 128:(cc + 1) * 128, :],
                      waits=waits if cc == 0 else (), ring=ring)
        return tok


    def _finish_early():
        for i in range(NTT):
            dma("sync", dy[i * 128:(i + 1) * 128, :], stream[:, 0:1024],
                waits=(p.now("sp"), p.now("sv"), p.now("sa"), p.now("sc")))
        for tok in now_dma("sync") + now_dma("gpsimd"):
            p._wait("sync", tok)
        with nc.Block() as block:
            @block.sync
            def _(e):
                p.emit(e, "sync", handles)
            @block.gpsimd
            def _(e):
                p.emit(e, "gpsimd", handles)
            @block.tensor
            def _(e):
                p.emit(e, "tensor", handles)
            @block.vector
            def _(e):
                p.emit(e, "vector", handles)
            @block.scalar
            def _(e):
                p.emit(e, "scalar", handles)
        st.close()
        return nc, tap_handles

    ev_done = {}
    t_stream = None
    for tt_i in range(NTT):
        slot = temps[:, (tt_i % 2) * 1024:(tt_i % 2) * 1024 + 1024]
        tok_load = dma("sync", slot, dx[tt_i * 128:(tt_i + 1) * 128, :],
                       waits=(ev_done.get(tt_i - 2),), ring="sx0" if tt_i % 2 == 0 else "sx1")
        for half in range(2):
            tr_last = None
            for j in range(4):
                c = half * 4 + j
                if tt_i % 2 == 0:
                    bsl = (PS1 if half == 0 else PS2)[:, j * 128:j * 128 + 128]
                else:
                    bsl = PS0[:, half * 512 + j * 128: half * 512 + j * 128 + 128]
                tr_last = p.run("tensor",
                                lambda e, o=bsl, i=slot[:, c * 128:(c + 1) * 128]:
                                e.transpose(o, i, identf[:]),
                                waits=(tok_load, tok_consts,
                                       ev_done.get((tt_i - 2, half)),
                                       t_btr if tt_i < 2 else None),
                                inc="sp")
            for j in range(4):
                c = half * 4 + j
                if tt_i % 2 == 0:
                    bsl = (PS1 if half == 0 else PS2)[:, j * 128:j * 128 + 128]
                else:
                    bsl = PS0[:, half * 512 + j * 128: half * 512 + j * 128 + 128]
                t_stream = act(chunk(c, tt_i * 128, tt_i * 128 + 128), bsl, AF.Copy,
                               waits=(tr_last, t_bias if tt_i < 2 else None))
            ev_done[(tt_i, half)] = t_stream
        ev_done[tt_i] = t_stream

    if upto == "p1":
        return _finish_early()

    def ln_full(out_mode, guard, pre_cast=None):
        if guard is None:
            gt = ()
        elif isinstance(guard[0], str):
            gt = (guard,)
        else:
            gt = guard
        if pre_cast is not None:
            cast_toks = pre_cast
        else:
            toks = [vcopy(xb(c), chunk(c), waits=gt) for c in range(CH)]
            cast_toks = {(tg, c): toks[c] for tg in range(TGN) for c in range(CH)}
        sqmm_tok = {}
        e1_tok = {}
        var_tok = {}
        out_toks = []
        for tg in range(TGN):
            lo, hi = tg * TGS, (tg + 1) * TGS
            m1 = m2 = None
            for c in range(CH):
                sslot = stgA[:, (c % 4) * 512:(c % 4) * 512 + 512]
                tok_sq = act(sslot, xb(c, lo, hi), AF.Square,
                             waits=(cast_toks[(tg, c)],) + gt +
                                   (sqmm_tok.get((tg, c - 4)), sqmm_tok.get((tg - 1, c + 4))))
                m1 = mm(PS1[:, 0:512], meanpadb[:], xb(c, lo, hi), c == 0, c == CH - 1,
                        waits=(cast_toks[(tg, c)], e1_tok.get(tg - 1)) + gt,
                        inc="sp")
                m2 = mm(PS2[:, 0:512], meanpadb[:], sslot, c == 0, c == CH - 1,
                        waits=(tok_sq, var_tok.get(tg - 1)), inc="sp")
                sqmm_tok[(tg, c)] = m2
            e1 = act(rw(RW_M, TGS), PS1[0:1, 0:512], AF.Copy, waits=(m1,))
            e1_tok[tg] = e1
            v1 = tt(rw(RW_V, TGS), rw(RW_M, TGS), rw(RW_M, TGS), ALU.mult, waits=(e1,))
            v2 = tt(rw(RW_V, TGS), PS2[0:1, 0:512], rw(RW_V, TGS), ALU.subtract,
                    waits=(v1, m2))
            var_tok[tg] = v2
            v3 = act(rw(RW_V, TGS), rw(RW_V, TGS), AF.Ln, bias=EPS, waits=(v2,))
            v4 = act(rw(RW_A, TGS), rw(RW_V, TGS), AF.Exp, scale=-0.5, waits=(v3,))
            v5 = tt(rw(RW_B, TGS), rw(RW_M, TGS), rw(RW_A, TGS), ALU.mult, waits=(v4,))
            prev = tuple(out_toks[-CH:])
            ta = expand(rw(RW_A, TGS), PS3, TGS, waits=(v4,) + prev)
            tb = expand(rw(RW_B, TGS), PS4, TGS, waits=(v5,) + prev)
            for c in range(CH):
                t1 = tt(temps[:, 1536:2048], xb(c, lo, hi), PS3[:, 0:TGS], ALU.mult,
                        waits=(ta, cast_toks[(tg, c)], p.now("sv")))
                dst = xb(c, lo, hi) if out_mode == "xb" else chunk(c, lo, hi)
                out_toks.append(tt(dst, temps[:, 1536:2048], PS4[:, 0:TGS],
                                   ALU.subtract, waits=(tb, t1)))
        return out_toks

    t_h1 = ln_full("xb", t_stream)[-1]

    if "h1" in taps:
        for c in range(CH):
            dma("sync", tap_handles["h1"][c * 128:(c + 1) * 128, :],
                xbA[:].rearrange("p (c t) -> p c t", c=CH)[:, c, :], waits=(t_h1,), ring="st")

    if upto == "ln1":
        return _finish_early()

    def fm_proj_phase(base, bias_col, scale, dst_dram, w_tok, mode, bank_guard,
                      prev_dma=None):
        ev = {}
        sq_act = {}
        sq_mm = {}
        dma_toks = {}

        def emit_sq(i):
            et_, tg_ = i // TGN, i % TGN
            sslot = stgA[:, 512 + (i % 2) * 1024: 512 + (i % 2) * 1024 + 512]
            stg_ = stgA[:, (i % 2) * 1024:(i % 2) * 1024 + 512]
            sq_act[i] = act(sslot, stg_, AF.Square,
                            waits=(ev[i], sq_mm.get(i - 2)))
            sq_mm[i] = mm(PS0[:, tg_ * 512: tg_ * 512 + 512], onespadb[:], sslot,
                          et_ == 0, et_ == CH - 1,
                          waits=(sq_act[i],) + ((bank_guard,) if i < TGN else ()),
                          inc="sp")

        idx = 0
        for et in range(CH):
            for tg in range(TGN):
                lo, hi = tg * TGS, (tg + 1) * TGS
                bank = PS1 if idx % 2 == 0 else PS2
                last = None
                for cc in range(CH):
                    w = (w_tok, t_h1, ev.get(idx - 2), bank_guard if idx == 0 else None)
                    last = mm(bank[:, 0:512],
                              wqA[:, base + cc * D + et * 128: base + cc * D + et * 128 + 128],
                              xb(cc, lo, hi), cc == 0, cc == CH - 1,
                              waits=w if cc == 0 else (), inc="sp" if cc == CH - 1 else None)
                stg = stgA[:, (idx % 2) * 1024:(idx % 2) * 1024 + 512]
                ev[idx] = act(stg, bank[:, 0:512], AF.Identity,
                              bias=bcol(bias_col + et), scale=scale,
                              waits=(last, t_bq32,
                                     dma_toks.get(idx - 2,
                                                  prev_dma[idx % 2] if prev_dma else None),
                                     sq_act.get(idx - 2)))
                dma_toks[idx] = dma("sync", dst_dram[et * 128:(et + 1) * 128, lo:hi],
                                    stg, waits=(ev[idx],),
                                    ring="ss0" if idx % 2 == 0 else "ss1")
                if idx >= 1:
                    emit_sq(idx - 1)
                idx += 1
        emit_sq(idx - 1)
        row_toks = []
        for tg in range(TGN):
            wmm = (sq_mm[(CH - 1) * TGN + tg],)
            if mode == "ksq":
                row_toks.append(p.run(
                    "vector",
                    lambda e, o=rw(RW_MISC + tg, 1), i=PS0[0:1, tg * 512: tg * 512 + 512]:
                    e.reduce_max(o, i, axis=AX.X),
                    waits=wmm, inc="sv"))
            else:
                row_toks.append(act(qsqB[0:1, tg * TGS:(tg + 1) * TGS],
                                    PS0[0:1, tg * 512: tg * 512 + 512], AF.Copy,
                                    waits=wmm))
        return ev[idx - 1], (dma_toks[idx - 2], dma_toks[idx - 1]), row_toks

    t_wk = load_w_section(dwqkv[:, D:2 * D], D, 0, waits=(tok_consts,), ring="w0")
    t_wv = load_w_section(dwqkv[:, 2 * D:3 * D], D, 8192, waits=(tok_consts,), ring="w1")
    k_ev, k_dmas, k_sq = fm_proj_phase(0, BC_BK, 1.0, kag_in, t_wk, "ksq", t_stream)
    k_dma = k_dmas[1]
    k_sp = p.now("sp")

    t_kmax = p.run("vector",
                   lambda e: e.reduce_max(rw(RW_MISC + 4, 1), rw(RW_MISC, 4), axis=AX.X),
                   waits=tuple(k_sq), inc="sv")
    t_kb = vcopy(rw(RW_MISC + 8, 8), rw(RW_MISC + 4, 1).to_broadcast((1, 8)),
                 waits=(t_kmax,))
    t_kd = dma("gpsimd", kmax_in[:], rw(RW_MISC + 8, 8), waits=(t_kb,), ring="sk")

    v_ev = {}
    v_dma = {}
    idx = 0
    for tt_i in range(NTT):
        for eg in range(2):
            bank = PS1 if idx % 2 == 0 else PS2
            last = None
            for cc in range(CH):
                w = (t_wv, t_h1, v_ev.get(idx - 2), k_ev if idx == 0 else None)
                last = mm(bank[:, 0:512],
                          xb(cc, tt_i * 128, (tt_i + 1) * 128),
                          wqA[:, 8192 + cc * D + eg * 512: 8192 + cc * D + eg * 512 + 512],
                          cc == 0, cc == CH - 1,
                          waits=w if cc == 0 else (), inc="sp" if cc == CH - 1 else None)
            stg = stgA[:, (idx % 2) * 1024:(idx % 2) * 1024 + 512]
            v_ev[idx] = act(stg, bank[:, 0:512], AF.Copy,
                            waits=(last, v_dma.get(idx - 2, k_dmas[idx % 2])))
            v_dma[idx] = dma("sync",
                             vag_in[tt_i * 128:(tt_i + 1) * 128, eg * 512:(eg + 1) * 512],
                             stg, waits=(v_ev[idx],),
                             ring="ss0" if idx % 2 == 0 else "ss1")
            idx += 1
    v_last_ev = v_ev[idx - 1]
    v_last_dma = v_dma[idx - 1]

    if skip_cc:
        t_ar_kmax = dma("gpsimd", kmax_out[:], kmax_in[:], waits=(t_kd,))
    else:
        t_ar_kmax = gp(lambda e: e.collective_compute(
            "AllReduce", ALU.max, replica_groups=[[0, 1, 2, 3], [4, 5, 6, 7]],
            ins=[kmax_in[:]], outs=[kmax_out[:]]),
            waits=(t_kd, k_dmas[0], k_dmas[1]), inc="sc")
    t_kmg = dma("gpsimd", rw(RW_MISC + 16, 8), kmax_out[:], waits=(t_ar_kmax,), ring="sk")

    if skip_cc:
        t_ag_k = dma("gpsimd", kag_out_r(0), kag_in, waits=(k_dma,))
        t_ag_v = dma("gpsimd", vag_out_r(0), vag_in, waits=(v_last_dma,))
    else:
        t_ag_k = gp(lambda e: e.collective_compute(
            "AllGather", ALU.bypass, replica_groups=[[0, 1, 2, 3], [4, 5, 6, 7]],
            ins=[kvag_in_t[:]], outs=[kvag_out_t[:]]),
            waits=(t_kmg, v_dma[idx - 2], v_last_dma), inc="sc")
        t_ag_v = t_ag_k
    if "kag" in taps:
        dma("sync", tap_handles["kag"][:], kag_in, waits=(k_dma, t_ag_k), ring="st")
    if "vag" in taps:
        dma("sync", tap_handles["vag"][:], vag_in, waits=(v_last_dma, t_ag_v), ring="st")

    t_wq = load_w_section(dwqkv[:, 0:D], D, 0, waits=(k_sp,), ring="w2")
    q_ev, q_dmas, q_sq = fm_proj_phase(0, BC_BQ, 1.0 / 32.0, qt_dram, t_wq, "qsq",
                                      v_last_ev, prev_dma=(v_dma[idx - 2], v_dma[idx - 1]))
    q_dma = q_dmas[1]
    if "qt" in taps:
        dma("sync", tap_handles["qt"][:], qt_dram[:], waits=(q_dma,), ring="st")

    t_wproj = load_w_section(dwproj, D, 8192, waits=(p.now("sp"),), ring="w3")
    t_cn1 = load_w_section(dcw1, D, 0, waits=(q_dma, p.now("sp")), ring="w0")

    if upto == "p3":
        return _finish_early()

    xf_sp = None
    for c in range(CH):
        xf_sp = dma("sync", xfm_dram[c * 128:(c + 1) * 128, :], chunk(c),
                    waits=(t_h1,), ring="sxf")

    q_ld = None
    for cc in range(CH):
        q_ld = dma("sync", xbA[:, cc * T:(cc + 1) * T],
                   qt_dram[cc * 128:(cc + 1) * 128, :],
                   waits=(q_dmas[0], q_dmas[1], q_ev), ring="qld")

    def qT(cc, qlo, qhi):
        return xbA[:, cc * T + qlo: cc * T + qhi]

    def emb_sl(qt, lo=0, hi=8 * QTS):
        return stream[:, qt * (CH * QTS) + lo: qt * (CH * QTS) + hi]

    sb1 = tt(rw(0, 1024), qsqB[0:1, 0:1024],
             rw(RW_MISC + 16, 1).to_broadcast((1, 1024)), ALU.mult,
             waits=(t_kmg, q_sq[-1], p.now("sv")))
    sb1b = tt(rw(1024, 1024), qsqB[0:1, 1024:2048],
              rw(RW_MISC + 16, 1).to_broadcast((1, 1024)), ALU.mult, waits=(sb1,))
    sb2 = act(rw(0, 1024), rw(0, 1024), AF.Ln, waits=(sb1,))
    sb2 = act(rw(0, 1024), rw(0, 1024), AF.Exp, scale=0.5, waits=(sb2,))
    sb2b = act(rw(1024, 1024), rw(1024, 1024), AF.Ln, waits=(sb1b,))
    sb2b = act(rw(1024, 1024), rw(1024, 1024), AF.Exp, scale=0.5, waits=(sb2b,))
    sb3 = ts2(rw(0, 1024), rw(0, 1024), -1.01, ALU.mult, waits=(sb2,))
    sb3b = ts2(rw(1024, 1024), rw(1024, 1024), -1.01, ALU.mult, waits=(sb2b,))
    t_pad = vcopy(padb[0:1, 0:1024], rw(0, 1024), waits=(sb3,))
    t_pad = vcopy(padb[0:1, 1024:2048], rw(1024, 1024), waits=(sb3b, t_pad))

    def sums_sl(qt):
        return rows[0:1, qt * QTS:(qt + 1) * QTS]

    use_k = {}
    use_v = {}
    exp_guard = {}
    evac_tok = {}
    sums_tok = {}
    for kb in range(NKB):
        r, ts_off = kb // 4, (kb % 4) * KBS
        par = kb % 2
        kslot = kvA[:, par * 4096: par * 4096 + 4096]
        vslot = kvA[:, 8192:12288]
        tk = dma("sync", kslot.rearrange("p (cc t) -> p cc t", cc=CH),
                 kag_out_r(r).rearrange("(cc p) t -> p cc t", p=128)[:, :, ts_off:ts_off + KBS],
                 waits=(t_ag_v, use_k.get(kb - 2)),
                 ring="kk0" if kb % 2 == 0 else "kk1")
        tv = dma("sync", vslot.rearrange("p (kt d) -> p kt d", kt=4),
                 vag_out_r(r)[ts_off:ts_off + KBS, :].rearrange("(kt p) d -> p kt d", p=128),
                 waits=(t_ag_v, use_v.get(kb - 1)), ring="kv0")
        for qt_i in range(NQT):
            qlo = qt_i * QTS
            exp_toks = []
            stab_tok = {}
            for kt in range(4):
                bank = PS1 if kt < 2 else PS2
                boff = (kt % 2) * 256
                w0 = (tk, q_ld, t_pad, exp_guard.get(kt // 2))
                for cc in range(CH):
                    mm(bank[:, boff:boff + 256],
                       kvA[:, par * 4096 + cc * 512 + kt * 128: par * 4096 + cc * 512 + kt * 128 + 128],
                       qT(cc, qlo, qlo + QTS),
                       cc == 0, False, waits=w0 if cc == 0 else ())
                stab_tok[kt] = mm(bank[:, boff:boff + 256], e0b[:],
                                  padb[:, qlo:qlo + QTS], False, True, inc="sp")
                if kt % 2 == 1:
                    dst = prA[:, (kt // 2) * 512: (kt // 2) * 512 + 512]
                    etok = act(dst, bank[:, 0:512], AF.Exp,
                               waits=(stab_tok[kt - 1], stab_tok[kt]))
                    exp_toks.append(etok)
                    exp_guard[kt // 2] = etok
            if qt_i == NQT - 1:
                use_k[kb] = stab_tok[3]
            av_last = None
            for dt in range(CH):
                for kt in range(4):
                    pr = prA[:, kt * 256: kt * 256 + 256]
                    w = ()
                    if kt == 0:
                        w = (tv, exp_toks[0], evac_tok.get((kb, qt_i - 1) if qt_i > 0 else (kb - 1, NQT - 1)))
                    elif kt == 2:
                        w = (exp_toks[1],)
                    av_last = mm(PS0[:, dt * 256: dt * 256 + 256],
                                 kvA[:, 8192 + kt * 1024 + dt * 128: 8192 + kt * 1024 + dt * 128 + 128],
                                 pr, (kt == 0 and dt % 2 == 0), False,
                                 waits=w, inc="sp" if kt == 3 and dt == CH - 1 else None)
            sums_last = None
            for kt in range(4):
                pr = prA[:, kt * 256: kt * 256 + 256]
                w = ()
                if kt == 0:
                    w = (exp_toks[0], sums_tok.get((kb, qt_i - 1) if qt_i > 0 else (kb - 1, NQT - 1)))
                elif kt == 2:
                    w = (exp_toks[1],)
                sums_last = mm(PS3[:, 0:256], onespadb[:], pr,
                               kt == 0, kt == 3,
                               waits=w, inc="sp" if kt == 3 else None)
            if qt_i == NQT - 1:
                use_v[kb] = av_last
            if kb == 0:
                evac_tok[(kb, qt_i)] = vcopy(emb_sl(qt_i), PS0[:, 0:2048],
                                             waits=(av_last, xf_sp, p.now("sv")))
                sums_tok[(kb, qt_i)] = vcopy(sums_sl(qt_i), PS3[0:1, 0:256],
                                             waits=(sums_last,))
            else:
                evac_tok[(kb, qt_i)] = tt(emb_sl(qt_i), PS0[:, 0:2048], emb_sl(qt_i),
                                          ALU.add, waits=(av_last, p.now("sv")))
                sums_tok[(kb, qt_i)] = tt(sums_sl(qt_i), PS3[0:1, 0:256], sums_sl(qt_i),
                                          ALU.add, waits=(sums_last,))

    x1_last = {}
    t1_done = {}
    k_end = evac_tok[(NKB - 1, NQT - 1)]
    s_end = sums_tok[(NKB - 1, NQT - 1)]
    bgq = {"R0": k_end, "R1": k_end, "A0": k_end, "A1": k_end,
           "B0": s_end, "B1": None, "PS1": None, "PS2": None, "emA": None}
    def xdst(qt, lo=0, hi=8 * QTS):
        if qt == 0:
            return temps[:, lo:hi]
        return emb_sl(qt - 1, lo, hi)
    for qt_i in range(NQT):
        qlo = qt_i * QTS
        xload = None
        for c in range(CH):
            xload = dma("sync", xdst(qt_i, c * QTS, (c + 1) * QTS),
                        xfm_dram[c * 128:(c + 1) * 128, qlo:qlo + QTS],
                        waits=(xf_sp, t1_done.get(qt_i - 1),
                               x1_last.get(qt_i - 2)),
                        ring="xq0" if qt_i % 2 == 0 else "xq1")
        pad_off = (qt_i % 2) * 256
        bcR = PS0[:, (qt_i % 2) * 512: (qt_i % 2) * 512 + QTS]
        bcA = PS0[:, 1024 + (qt_i % 2) * 512: 1024 + (qt_i % 2) * 512 + QTS]
        bcB = PS3[:, 0:QTS] if qt_i % 2 == 0 else PS4[:, 0:QTS]
        wv_l = act(sums_sl(qt_i), sums_sl(qt_i), AF.Ln,
                   waits=(sums_tok[(NKB - 1, qt_i)],))
        wv_r = act(pads[0:1, pad_off:pad_off + QTS], sums_sl(qt_i), AF.Exp,
                   scale=-1.0, waits=(wv_l, last_pad_mm[qt_i % 2]))
        t_bc = mm(bcR, e0f[:], pads[:, pad_off:pad_off + QTS], True, True,
                  waits=(wv_r, bgq.get("R%d" % (qt_i % 2))), inc="sp")
        last_pad_mm[qt_i % 2] = t_bc
        emb_toks = []
        t1_last = None
        for dt in range(CH):
            t1_last = tt(attnf[:, dt * 256:(dt + 1) * 256],
                         emb_sl(qt_i, dt * 256, dt * 256 + 256), bcR,
                         ALU.mult,
                         waits=(t_bc, p.now("sv"),
                                bgq.get("apG") if dt == 0 else None))
            emb_toks.append(act(emA[:, dt * 256:(dt + 1) * 256],
                                attnf[:, dt * 256:(dt + 1) * 256], AF.Identity,
                                bias=bcol(BC_BV + dt),
                                waits=(t1_last, bgq.get("emA"))))
        proj_toks = []
        for et in range(CH):
            bank = PS1 if et % 2 == 0 else PS2
            last = None
            for cc in range(CH):
                w = (t_wproj, emb_toks[-1],
                     proj_toks[et - 2] if et >= 2 else
                     (exp_guard.get(et) if qt_i == 0 else bgq.get("PS1" if et == 0 else "PS2")))
                last = mm(bank[:, 0:256],
                          wqA[:, 8192 + cc * D + et * 128: 8192 + cc * D + et * 128 + 128],
                          emA[:, cc * 256:(cc + 1) * 256],
                          cc == 0, cc == CH - 1,
                          waits=w if cc == 0 else (), inc="sp" if cc == CH - 1 else None)
            proj_toks.append(act(attnf[:, et * 256:(et + 1) * 256], bank[:, 0:256],
                                 AF.Identity, bias=bcol(BC_BP + et),
                                 waits=(last, t1_last if et < 2 else None)))
        t_cast = vcopy(stgA[:, 0:2048].rearrange("p (a b) -> p a b", a=1) if False else stgA[:, 0:2048],
                       attnf[:], waits=(proj_toks[-1], p.now("ss0"), p.now("ss1")))
        t_sq = act(emA[:, 0:2048], attnf[:], AF.Square,
                   waits=(proj_toks[-1], emb_toks[-1]))
        m1 = m2 = None
        for cc in range(CH):
            m1 = mm(PS1[:, 0:256], meanpadb[:], stgA[:, cc * 256:(cc + 1) * 256],
                    cc == 0, cc == CH - 1,
                    waits=(t_cast, proj_toks[-2]) if cc == 0 else (),
                    inc="sp" if cc == CH - 1 else None)
        for cc in range(CH):
            m2 = mm(PS2[:, 0:256], meanpadb[:], emA[:, cc * 256:(cc + 1) * 256],
                    cc == 0, cc == CH - 1,
                    waits=(t_sq, proj_toks[-1]) if cc == 0 else (),
                    inc="sp" if cc == CH - 1 else None)
        e1 = act(rw(RW2_M, QTS), PS1[0:1, 0:256], AF.Copy, waits=(m1,))
        v1 = tt(rw(RW2_V, QTS), rw(RW2_M, QTS), rw(RW2_M, QTS), ALU.mult, waits=(e1,))
        v2 = tt(rw(RW2_V, QTS), PS2[0:1, 0:256], rw(RW2_V, QTS), ALU.subtract,
                waits=(v1, m2))
        v3 = act(rw(RW2_V, QTS), rw(RW2_V, QTS), AF.Ln, bias=EPS, waits=(v2,))
        v4 = act(rw(RW2_A, QTS), rw(RW2_V, QTS), AF.Exp, scale=-0.5, waits=(v3,))
        v5 = tt(rw(RW2_B, QTS), rw(RW2_M, QTS), rw(RW2_A, QTS), ALU.mult, waits=(v4,))
        ta = mm(bcA, e0f[:], pads[:, pad_off:pad_off + QTS], True, True,
                waits=(vcopy(pads[0:1, pad_off:pad_off + QTS], rw(RW2_A, QTS),
                             waits=(v4, last_pad_mm[qt_i % 2])),
                       bgq.get("A%d" % (qt_i % 2))), inc="sp")
        last_pad_mm[qt_i % 2] = ta
        tb = mm(bcB, e0f[:], pads[:, pad_off:pad_off + QTS], True, True,
                waits=(vcopy(pads[0:1, pad_off:pad_off + QTS], rw(RW2_B, QTS),
                             waits=(v5, last_pad_mm[qt_i % 2])),
                       bgq.get("B%d" % (qt_i % 2))), inc="sp")
        last_pad_mm[qt_i % 2] = tb
        ap = None
        u1_latest = None
        for cc in range(CH):
            u1_latest = tt(attnf[:, cc * 256:(cc + 1) * 256],
                           attnf[:, cc * 256:(cc + 1) * 256], bcA,
                           ALU.mult, waits=(ta, t_cast, t_sq, p.now("sv")))
            u2 = tt(attnf[:, cc * 256:(cc + 1) * 256],
                    attnf[:, cc * 256:(cc + 1) * 256], bcB,
                    ALU.subtract, waits=(tb, u1_latest))
            ap = p.run("gpsimd",
                       lambda e, o=xdst(qt_i, cc * QTS, (cc + 1) * QTS),
                       a=attnf[:, cc * 256:(cc + 1) * 256]:
                       e.tensor_tensor(o, o, a, ALU.add),
                       waits=(u2, xload), inc="so")
        x1d = None
        for c in range(CH):
            x1d = dma("sync", x1_dram[c * 128:(c + 1) * 128, qlo:qlo + QTS],
                      xdst(qt_i, c * QTS, (c + 1) * QTS), waits=(ap,),
                      ring="x1q0" if qt_i % 2 == 0 else "x1q1")
        x1_last[qt_i] = x1d
        t1_done[qt_i] = t1_last
        bgq["PS1"] = e1
        bgq["PS2"] = v2
        bgq["emA"] = m2
        bgq["R%d" % (qt_i % 2)] = t1_last
        bgq["A%d" % (qt_i % 2)] = u1_latest
        bgq["B%d" % (qt_i % 2)] = u2
        bgq["apG"] = ap

    t_x1a = t_x1b = None
    for c in range(CH):
        eng_r = ("sync", "sx0" if c % 4 == 0 else "sx1") if c % 2 == 0             else ("gpsimd", "cs0" if c % 4 == 1 else "cs1")
        tok = dma(eng_r[0], chunk(c), x1_dram[c * 128:(c + 1) * 128, :],
                  waits=(x1_last[NQT - 1], x1_last[NQT - 2],
                         evac_tok[(NKB - 1, 7)], p.now("sv")),
                  ring=eng_r[1])
    t_x1 = (p.now("sx0"), p.now("sx1"), p.now("cs0"), p.now("cs1"))
    x1_spill = x1_last[NQT - 1]

    if "x1" in taps:
        for c in range(CH):
            dma("sync", tap_handles["x1"][c * 128:(c + 1) * 128, :], chunk(c),
                waits=t_x1, ring="st")

    if upto == "attn":
        return _finish_early()

    ln3 = ln_full("xb", t_x1)
    ln3_tg = [ln3[tg * CH + CH - 1] for tg in range(TGN)]
    t_h2 = ln3[-1]
    if "h2" in taps:
        for c in range(CH):
            dma("sync", tap_handles["h2"][c * 128:(c + 1) * 128, :],
                xbA[:].rearrange("p (c t) -> p c t", c=CH)[:, c, :], waits=(t_h2,), ring="st")

    ctx_tok = None
    for c in range(CH):
        ctx_tok = p.run("vector",
                        lambda e, o=ctxP[:, c:c + 1], i=xb(c):
                        e.reduce_sum(o, i, axis=AX.X),
                        waits=(t_h2,), inc="sv")
    t_ctxd = dma("gpsimd", ctx_in[:], ctxP[:, 0:CH], waits=(ctx_tok,), ring="sk")
    if skip_cc:
        t_ar_ctx = dma("gpsimd", ctx_out[:], ctx_in[:], waits=(t_ctxd,))
    else:
        t_ar_ctx = gp(lambda e: e.collective_compute(
            "AllReduce", ALU.add, replica_groups=[[0, 1, 2, 3], [4, 5, 6, 7]],
            ins=[ctx_in[:]], outs=[ctx_out[:]]),
        waits=(t_ctxd, x1_spill) + now_dma("sync") + now_dma("gpsimd"), inc="sc")
    t_ctxl = dma("gpsimd", ctxP[:, 8:16], ctx_out[:], waits=(t_ar_ctx,), ring="sk")
    t_ctxs = ts2(ctxP[:, 8:16], ctxP[:, 8:16], 1.0 / S, ALU.mult, waits=(t_ctxl,))
    if "ctx" in taps:
        dma("sync", tap_handles["ctx"][:], ctxP[:, 8:16], waits=(t_ctxs,), ring="st")
    ctxb = stgA[:, 0:8]
    t_ctxb = vcopy(ctxb, ctxP[:, 8:16], waits=(t_ctxs, p.now("sp")))

    if upto == "ln3":
        return _finish_early()

    t_cn2 = load_w_section(dcw2, 512, 8192, waits=(p.now("sp"),), ring="w1")
    t_cn3 = load_w_section(dcw3, 256, 12288, waits=(t_cn2,), ring="w2")
    mz1 = gp(lambda e: e.memset(wqA[:, 13312:13568], 0.0), waits=(t_cn3,),
             inc="so", inck=1)
    t_cn4a = dma("gpsimd", wqA[:, 13312:13313], dcw4[0:128, :], waits=(mz1,), ring="w3")
    t_cn4 = dma("gpsimd", wqA[:, 13440:13441], dcw4[128:256, :], ring="w3")
    t_gc2 = load_w_section(dgw2, 256, 13568, waits=(t_cn4,), ring="w0")
    mz2 = gp(lambda e: e.memset(wqA[:, 14592:14848], 0.0), waits=(t_gc2,),
             inc="so", inck=1)
    t_gc3a = dma("gpsimd", wqA[:, 14592:14593], dgw3[0:128, :], waits=(mz2,), ring="w1")
    t_gc3 = dma("gpsimd", wqA[:, 14720:14721], dgw3[128:256, :], ring="w1")
    t_gc1 = load_w_section(dgw1, 512, 7168, region=kvA, waits=(p.now("sp"),), ring="w2")
    t_wffn = (p.now("w0"), p.now("w1"), p.now("w2"), p.now("w3"))

    gc_toks = {}
    def emit_gc():
        gc1_toks = []
        for et in range(4):
            last = None
            for cc in range(CH):
                last = mm(PS0[:, et * 512:et * 512 + 1],
                          kvA[:, 7168 + cc * 512 + et * 128: 7168 + cc * 512 + et * 128 + 128],
                          ctxb[:, cc:cc + 1], cc == 0, cc == CH - 1,
                          waits=t_wffn + (t_ctxb,) if cc == 0 else (),
                          inc="sp" if cc == CH - 1 else None)
            gc1_toks.append(act(emA[:, 1024 + et:1024 + et + 1], PS0[:, et * 512:et * 512 + 1],
                                AF.Gelu, bias=bcol(BC_GB1 + et), waits=(last,)))
        gc2_toks = []
        for et in range(2):
            last = None
            for cc in range(4):
                last = mm(PS0[:, et * 512 + 256:et * 512 + 257],
                          wqA[:, 13568 + cc * 256 + et * 128: 13568 + cc * 256 + et * 128 + 128],
                          emA[:, 1024 + cc:1024 + cc + 1], cc == 0, cc == 3,
                          waits=t_wffn + (gc1_toks[-1],) if cc == 0 else (),
                          inc="sp" if cc == 3 else None)
            gc2_toks.append(act(emA[:, 1032 + et:1032 + et + 1], PS0[:, et * 512 + 256:et * 512 + 257],
                                AF.Gelu, bias=bcol(BC_GB2 + et), waits=(last,)))
        last = None
        for cc in range(2):
            last = mm(PS0[:, 1024:1025], wqA[:, 14592 + cc * 128:14592 + cc * 128 + 128],
                      emA[:, 1032 + cc:1032 + cc + 1], cc == 0, cc == 1,
                      waits=t_wffn + (gc2_toks[-1],) if cc == 0 else (),
                      inc="sp" if cc == 1 else None)
        t_gf = act(bcol(BC_GF, 0, 1), PS0[0:1, 1024:1025], AF.Sigmoid,
                   bias=bcol(BC_GB3, 0, 1), waits=(last,))
        s1 = ts2(bcol(BC_T1, 0, 1), bcol(BC_GF, 0, 1), 1.4, ALU.mult, waits=(t_gf,))
        s2 = tt(bcol(BC_T1, 0, 1), bcol(BC_T1, 0, 1), bcol(BC_AR, 0, 1), ALU.mult,
                waits=(s1, t_bias))
        s3 = tt(bcol(BC_T1, 0, 1), bcol(BC_T1, 0, 1), bcol(BC_AR, 0, 1), ALU.subtract,
                waits=(s2,))
        s4 = ts2(bcol(BC_SADD, 0, 1), bcol(BC_T1, 0, 1), 0.1, ALU.add, waits=(s3,))
        t_smul = ts2(bcol(BC_SMUL, 0, 1), bcol(BC_AR, 0, 1), 0.6, ALU.mult, waits=(t_bias,))

        gc_toks['s4'] = s4
        gc_toks['smul'] = t_smul

    cyc_done = t_h2
    ln4_casts = {}
    for cyc in range(3):
        if cyc > 0:
            src_tok_pair = (p.now("cs0"), p.now("cs1"))
        else:
            src_tok_pair = ()
        hp = {}
        apply_done = {}
        evt_toks = {}

        def emit_apply(tg, cyc=cyc, hp=hp, apply_done=apply_done, evt_toks=evt_toks,
                       src_tok_pair=src_tok_pair):
            lo, hi = tg * TGS, (tg + 1) * TGS
            c1 = ts2(rw(RW_STG, TGS), tfB[0:1, lo:hi], bcol(BC_SMUL, 0, 1), ALU.mult,
                     bcol(BC_SADD, 0, 1), ALU.add,
                     waits=(gc_toks['s4'], gc_toks['smul'], evt_toks[tg]))
            prev = tuple(apply_done.get((tg - 1, c)) for c in range(CH))
            tc = expand(rw(RW_STG, TGS), PS3, TGS, waits=(c1,) + prev)
            c2 = ts2(rw(RW_STG, TGS), rw(RW_STG, TGS), -1.0, ALU.mult, waits=(tc,))
            tn = expand(rw(RW_STG, TGS), PS4, TGS, waits=(c2,) + prev)
            last = None
            for c in range(CH):
                src_c, sign = TWIST[c]
                bcp, bct = (PS3, tc) if sign > 0 else (PS4, tn)
                u1 = tt(temps[:, 1536:2048], xb(src_c, lo, hi), bcp[:, 0:TGS],
                        ALU.mult, waits=(bct, p.now("sv")) + src_tok_pair)
                base = xb(c, lo, hi) if cyc == 0 else chunk(c, lo, hi)
                last = tt(chunk(c, lo, hi), base, temps[:, 1536:2048], ALU.add,
                          waits=(u1, x1_spill if cyc == 0 else None))
                apply_done[(tg, c)] = last
            return last

        for tg in range(TGN):
            lo, hi = tg * TGS, (tg + 1) * TGS
            h1_toks = []
            for et in range(CH):
                bank = PS1 if et % 2 == 0 else PS2
                last = None
                for cc in range(CH):
                    w = t_wffn + src_tok_pair + (
                         ln3_tg[tg] if cyc == 0 else None,
                         hp.get(("h1", tg, et - 2)),
                         hp.get(("h4", tg - 1)) if et < 2 else None)
                    last = mm(bank[:, 0:512],
                              wqA[:, cc * D + et * 128: cc * D + et * 128 + 128],
                              xb(cc, lo, hi), cc == 0, cc == CH - 1,
                              waits=w if cc == 0 else (),
                              inc="sp" if cc == CH - 1 else None)
                ev = act(kvA[:, et * 512:(et + 1) * 512], bank[:, 0:512], AF.Gelu,
                         bias=bcol(BC_CB1 + et), waits=(last,))
                hp[("h1", tg, et)] = ev
                h1_toks.append(ev)
            if cyc == 0 and tg == 0:
                emit_gc()
            h2_toks = []
            for et in range(4):
                bank = PS1 if et % 2 == 0 else PS2
                last = None
                for cc in range(CH):
                    w = t_wffn + (h1_toks[-1], hp.get(("h2", tg, et - 2)))
                    last = mm(bank[:, 0:512],
                              wqA[:, 8192 + cc * 512 + et * 128: 8192 + cc * 512 + et * 128 + 128],
                              kvA[:, cc * 512:(cc + 1) * 512], cc == 0, cc == CH - 1,
                              waits=w if cc == 0 else (),
                              inc="sp" if cc == CH - 1 else None)
                ev = act(kvA[:, 4096 + et * 512: 4096 + (et + 1) * 512], bank[:, 0:512],
                         AF.Gelu, bias=bcol(BC_CB2 + et), waits=(last,))
                hp[("h2", tg, et)] = ev
                h2_toks.append(ev)
            h3_toks = []
            for et in range(2):
                bank = PS1 if et % 2 == 0 else PS2
                last = None
                for cc in range(4):
                    w = t_wffn + (h2_toks[-1], hp.get(("h3", tg, et - 2)))
                    last = mm(bank[:, 0:512],
                              wqA[:, 12288 + cc * 256 + et * 128: 12288 + cc * 256 + et * 128 + 128],
                              kvA[:, 4096 + cc * 512: 4096 + (cc + 1) * 512],
                              cc == 0, cc == 3,
                              waits=w if cc == 0 else (), inc="sp" if cc == 3 else None)
                ev = act(kvA[:, 6144 + et * 512: 6144 + (et + 1) * 512], bank[:, 0:512],
                         AF.Gelu, bias=bcol(BC_CB3 + et), waits=(last,))
                hp[("h3", tg, et)] = ev
                h3_toks.append(ev)
            last = None
            for cc in range(2):
                w = t_wffn + (h3_toks[0], h3_toks[1], hp.get(("h4", tg - 1)))
                last = mm(PS1[:, 0:512],
                          wqA[:, 13312 + cc * 128: 13312 + cc * 128 + 128],
                          kvA[:, 6144 + cc * 512: 6144 + (cc + 1) * 512],
                          cc == 0, cc == 1,
                          waits=w if cc == 0 else (), inc="sp" if cc == 1 else None)
            evt = act(tfB[0:1, lo:hi], PS1[0:1, 0:512], AF.Sigmoid,
                      bias=bcol(BC_CB4, 0, 1), waits=(last,))
            hp[("h4", tg)] = evt
            evt_toks[tg] = evt
            if tg > 0:
                ap_tok = emit_apply(tg - 1)
                lo1, hi1 = (tg - 1) * TGS, tg * TGS
                if cyc == 2:
                    for c in range(CH):
                        ln4_casts[(tg - 1, c)] = vcopy(
                            xb(c, lo1, hi1), chunk(c, lo1, hi1), waits=(ap_tok,))
                else:
                    for c in range(CH):
                        dma("gpsimd", xb(c, lo1, hi1), chunk(c, lo1, hi1),
                            waits=(ap_tok,), ring="cs0" if c % 2 == 0 else "cs1")
        cyc_done = emit_apply(TGN - 1)
        lo1, hi1 = (TGN - 1) * TGS, TGN * TGS
        if cyc == 2:
            for c in range(CH):
                ln4_casts[(TGN - 1, c)] = vcopy(
                    xb(c, lo1, hi1), chunk(c, lo1, hi1), waits=(cyc_done,))
        else:
            for c in range(CH):
                dma("gpsimd", xb(c, lo1, hi1), chunk(c, lo1, hi1),
                    waits=(cyc_done,), ring="cs0" if c % 2 == 0 else "cs1")

    if "ffn" in taps:
        for c in range(CH):
            dma("sync", tap_handles["ffn"][c * 128:(c + 1) * 128, :], chunk(c),
                waits=(cyc_done,), ring="st")

    if upto == "ffn":
        return _finish_early()

    ln4 = ln_full("stream", cyc_done, pre_cast=ln4_casts)
    t_z = ln4[-1]
    add_toks = []
    tg_done = {}
    y_prev = {}
    ydma_prev = {}
    ti = 0
    for tg in range(TGN):
        lo, hi = tg * TGS, (tg + 1) * TGS
        for c in range(CH):
            slot = attnf[:, (c % 4) * 512:(c % 4) * 512 + 512]
            tl = dma("sync", slot, x1_dram[c * 128:(c + 1) * 128, lo:hi],
                     waits=(x1_spill, t_z,
                            add_toks[-4] if len(add_toks) >= 4 else None),
                     ring="xr%d" % (c % 4))
            add_toks.append(tt(chunk(c, lo, hi), chunk(c, lo, hi), slot, ALU.add,
                               waits=(tl, t_z)))
        tg_done[tg] = add_toks[-1]
        for tt_j in range(4):
            tt_i = tg * 4 + tt_j
            ystg = temps[:, (tt_i % 2) * 1024:(tt_i % 2) * 1024 + 1024]
            evs = []
            for half in range(2):
                tr_last = None
                for j in range(4):
                    c = half * 4 + j
                    if tt_i % 2 == 0:
                        bsl = (PS1 if half == 0 else PS2)[:, j * 128:j * 128 + 128]
                    else:
                        bsl = PS0[:, half * 512 + j * 128: half * 512 + j * 128 + 128]
                    tr_last = p.run("tensor",
                                    lambda e, o=bsl, i=chunk(c, tt_i * 128, (tt_i + 1) * 128):
                                    e.transpose(o, i, identf[:]),
                                    waits=(tg_done[tg], y_prev.get((tt_i - 2, half))),
                                    inc="sp")
                for j in range(4):
                    c = half * 4 + j
                    if tt_i % 2 == 0:
                        bsl = (PS1 if half == 0 else PS2)[:, j * 128:j * 128 + 128]
                    else:
                        bsl = PS0[:, half * 512 + j * 128: half * 512 + j * 128 + 128]
                    ev = act(ystg[:, c * 128:(c + 1) * 128], bsl, AF.Copy,
                             waits=(tr_last, ydma_prev.get(tt_i - 2)))
                    evs.append(ev)
                y_prev[(tt_i, half)] = evs[-1]
            ydma_prev[tt_i] = dma("sync", dy[tt_i * 128:(tt_i + 1) * 128, :], ystg,
                                  waits=tuple(evs[-8:]),
                                  ring="sy0" if tt_i % 2 == 0 else "sy1")
            ti += 1

    for tok in now_dma("sync") + now_dma("gpsimd"):
        p._wait("sync", tok)

    with nc.Block() as block:
        @block.sync
        def _(e):
            p.emit(e, "sync", handles)

        @block.gpsimd
        def _(e):
            p.emit(e, "gpsimd", handles)

        @block.tensor
        def _(e):
            p.emit(e, "tensor", handles)

        @block.vector
        def _(e):
            p.emit(e, "vector", handles)

        @block.scalar
        def _(e):
            p.emit(e, "scalar", handles)

    st.close()
    return nc, tap_handles


_CACHE = {}

W_KEYS = {
    "w_qkv": "L0_attn_w_qkv", "b_qkv": "L0_attn_b_qkv",
    "w_proj": "L0_attn_w_proj", "b_proj": "L0_attn_b_proj",
    "cn_w1": "L0_ffn_cn_w1", "cn_b1": "L0_ffn_cn_b1",
    "cn_w2": "L0_ffn_cn_w2", "cn_b2": "L0_ffn_cn_b2",
    "cn_w3": "L0_ffn_cn_w3", "cn_b3": "L0_ffn_cn_b3",
    "cn_w4": "L0_ffn_cn_w4", "cn_b4": "L0_ffn_cn_b4",
    "gc_w1": "L0_ffn_gc_w1", "gc_b1": "L0_ffn_gc_b1",
    "gc_w2": "L0_ffn_gc_w2", "gc_b2": "L0_ffn_gc_b2",
    "gc_w3": "L0_ffn_gc_w3", "gc_b3": "L0_ffn_gc_b3",
    "adaptive_range": "L0_ffn_adaptive_range",
}


def kernel(**inputs):
    x = np.ascontiguousarray(np.asarray(inputs["L0_attn_x"], dtype=np.float32))
    weights = {k: np.ascontiguousarray(np.asarray(inputs[v], dtype=np.float32))
               for k, v in W_KEYS.items()}
    if "nc" not in _CACHE:
        _CACHE["nc"] = build()[0]
    nc = _CACHE["nc"]
    in_maps = []
    for core in range(8):
        b, g = core // 4, core % 4
        m = dict(weights)
        m["x"] = np.ascontiguousarray(x[b, g * T:(g + 1) * T, :])
        in_maps.append(m)
    res = run_bass_kernel_spmd(nc, in_maps, core_ids=list(range(8)))
    y = np.empty((2, S, D), dtype=np.float32)
    for core in range(8):
        b, g = core // 4, core % 4
        y[b, g * T:(g + 1) * T, :] = res.results[core]["y"]
    return y


# version 11 (final)
# speedup vs baseline: 1.0127x; 1.0103x over previous
import numpy as np
import ml_dtypes
from contextlib import ExitStack

import concourse.bass as bass
import concourse.mybir as mybir
from concourse.bass_utils import run_bass_kernel_spmd

F32 = mybir.dt.float32
BF16 = mybir.dt.bfloat16
AF = mybir.ActivationFunctionType
ALU = mybir.AluOpType
AX = mybir.AxisListType

T = 2048
D = 1024
S = 8192
CH = 8
NTT = 16
TGN, TGS = 4, 512
NQT, QTS = 8, 256
NKB, KBS = 16, 512
EPS = 1e-5

ENGINES = ("sync", "gpsimd", "tensor", "vector", "scalar")
CC_BARRIER = True

RW_M, RW_V, RW_A, RW_B = 0, 512, 1024, 1536
RW_MISC = 2048
RW_STG = 2112
RW2_M, RW2_V, RW2_A, RW2_B = 2112, 2368, 2624, 2880
ROWS_COLS = 3136

BC_BQ, BC_BK, BC_BV, BC_BP = 0, 8, 16, 24
BC_CB1, BC_CB2, BC_CB3, BC_CB4 = 32, 40, 44, 46
BC_GB1, BC_GB2, BC_GB3, BC_AR = 47, 51, 53, 54
BC_T1, BC_SADD, BC_SMUL, BC_GF = 55, 56, 57, 58

TWIST = {0: (4, +1), 1: (5, +1), 2: (6, -1), 3: (7, -1),
         4: (0, -1), 5: (1, -1), 6: (2, +1), 7: (3, +1)}


class Prog:

    def __init__(self):
        self.ops = {e: [] for e in ENGINES}
        self.cnt = {}
        self.waited = {e: {} for e in ENGINES}

    def _wait(self, eng, tok):
        if tok is None:
            return
        s, v = tok
        if v <= 0 or self.waited[eng].get(s, 0) >= v:
            return
        self.waited[eng][s] = v
        self.ops[eng].append(("w", s, v))

    def run(self, eng, build, waits=(), inc=None, inck=1):
        for tok in waits:
            self._wait(eng, tok)
        if inc is not None:
            self.cnt[inc] = self.cnt.get(inc, 0) + inck
            tok = (inc, self.cnt[inc])
            self.ops[eng].append(("i", build, inc, inck))
            return tok
        self.ops[eng].append(("r", build))
        return None

    def now(self, sem):
        return (sem, self.cnt.get(sem, 0))

    def emit(self, eng_obj, eng_name, handles):
        for op in self.ops[eng_name]:
            if op[0] == "w":
                eng_obj.wait_ge(handles[op[1]], op[2])
            elif op[0] == "i":
                op[1](eng_obj).then_inc(handles[op[2]], op[3])
            else:
                op[1](eng_obj)


def build(taps=(), upto=None, skip_cc=False):
    nc = bass.Bass("TRN2", target_bir_lowering=False)
    taps = set(taps)

    dx = nc.dram_tensor("x", [T, D], F32, kind="ExternalInput")
    dwqkv = nc.dram_tensor("w_qkv", [D, 3 * D], F32, kind="ExternalInput")
    dbqkv = nc.dram_tensor("b_qkv", [3 * D], F32, kind="ExternalInput")
    dwproj = nc.dram_tensor("w_proj", [D, D], F32, kind="ExternalInput")
    dbproj = nc.dram_tensor("b_proj", [D], F32, kind="ExternalInput")
    dcw1 = nc.dram_tensor("cn_w1", [D, D], F32, kind="ExternalInput")
    dcb1 = nc.dram_tensor("cn_b1", [D], F32, kind="ExternalInput")
    dcw2 = nc.dram_tensor("cn_w2", [D, 512], F32, kind="ExternalInput")
    dcb2 = nc.dram_tensor("cn_b2", [512], F32, kind="ExternalInput")
    dcw3 = nc.dram_tensor("cn_w3", [512, 256], F32, kind="ExternalInput")
    dcb3 = nc.dram_tensor("cn_b3", [256], F32, kind="ExternalInput")
    dcw4 = nc.dram_tensor("cn_w4", [256, 1], F32, kind="ExternalInput")
    dcb4 = nc.dram_tensor("cn_b4", [1], F32, kind="ExternalInput")
    dgw1 = nc.dram_tensor("gc_w1", [D, 512], F32, kind="ExternalInput")
    dgb1 = nc.dram_tensor("gc_b1", [512], F32, kind="ExternalInput")
    dgw2 = nc.dram_tensor("gc_w2", [512, 256], F32, kind="ExternalInput")
    dgb2 = nc.dram_tensor("gc_b2", [256], F32, kind="ExternalInput")
    dgw3 = nc.dram_tensor("gc_w3", [256, 1], F32, kind="ExternalInput")
    dgb3 = nc.dram_tensor("gc_b3", [1], F32, kind="ExternalInput")
    dar = nc.dram_tensor("adaptive_range", [1], F32, kind="ExternalInput")
    dy = nc.dram_tensor("y", [T, D], F32, kind="ExternalOutput")

    qt_dram = nc.dram_tensor("qt_dram", [D, T], BF16)
    kvag_in_t = nc.dram_tensor("kvag_in", [2 * D * T], BF16)
    kvag_out_t = nc.dram_tensor("kvag_out", [4 * 2 * D * T], BF16)
    kag_in = kvag_in_t[0: D * T].rearrange("(d t) -> d t", t=T)
    vag_in = kvag_in_t[D * T: 2 * D * T].rearrange("(t d) -> t d", d=D)
    def kag_out_r(r):
        return kvag_out_t[r * 2 * D * T: r * 2 * D * T + D * T].rearrange(
            "(d t) -> d t", t=T)
    def vag_out_r(r):
        return kvag_out_t[r * 2 * D * T + D * T: (r + 1) * 2 * D * T].rearrange(
            "(t d) -> t d", d=D)
    kmax_in = nc.dram_tensor("kmax_in", [1, 8], F32)
    kmax_out = nc.dram_tensor("kmax_out", [1, 8], F32)
    ctx_in = nc.dram_tensor("ctx_in", [128, CH], F32)
    ctx_out = nc.dram_tensor("ctx_out", [128, CH], F32)
    x1_dram = nc.dram_tensor("x1_dram", [D, T], F32)
    xfm_dram = nc.dram_tensor("xfm_dram", [D, T], F32)

    tap_handles = {}
    def add_tap(name, shape, dtype):
        if name in taps:
            tap_handles[name] = nc.dram_tensor("tap_" + name, shape, dtype,
                                               kind="ExternalOutput")
    add_tap("h1", [D, T], BF16)
    add_tap("kag", [D, T], BF16)
    add_tap("vag", [T, D], BF16)
    add_tap("qt", [D, T], BF16)
    add_tap("x1", [D, T], F32)
    add_tap("h2", [D, T], BF16)
    add_tap("ffn", [D, T], F32)
    add_tap("ctx", [128, CH], F32)
    add_tap("x1d", [D, T], F32)
    add_tap("yfm", [D, T], F32)

    st = ExitStack()
    def sbt(name, shape, dtype):
        return st.enter_context(nc.sbuf_tensor(name, shape, dtype))
    def pst(name, shape):
        return st.enter_context(nc.psum_tensor(name, shape, F32))

    stream = sbt("stream", [128, CH * T], F32)
    temps = sbt("temps", [128, 2048], F32)
    rows = sbt("rows", [128, ROWS_COLS], F32)
    pads = sbt("pads", [128, 512], F32)
    attnf = sbt("attnf", [128, 2048], F32)
    biasar = sbt("biasar", [128, 64], F32)
    e0f = sbt("e0f", [128, 128], F32)
    identf = sbt("identf", [128, 128], F32)
    ctxP = sbt("ctxP", [128, 16], F32)
    cst = sbt("cst", [128, 2], F32)

    xbA = sbt("xbA", [128, CH * T], BF16)
    wqA = sbt("wqA", [128, 16384], BF16)
    kvA = sbt("kvA", [128, 12288], BF16)
    prA = sbt("prA", [128, 1024], BF16)
    emA = sbt("emA", [128, 2048], BF16)
    stgA = sbt("stgA", [128, 2048], BF16)
    padb = sbt("padb", [128, 2048], BF16)
    e0b = sbt("e0b", [128, 128], BF16)
    onespadb = sbt("onespadb", [128, 128], BF16)
    meanpadb = sbt("meanpadb", [128, 128], BF16)
    qsqB = sbt("qsqB", [128, 2048], BF16)
    tfB = sbt("tfB", [128, 2048], BF16)

    PS0 = pst("PS0", [128, 2048])
    PS1 = pst("PS1", [128, 512])
    PS2 = pst("PS2", [128, 512])
    PS3 = pst("PS3", [128, 512])
    PS4 = pst("PS4", [128, 512])

    sems = ["sd", "sg", "so", "sp", "sv", "sa", "sc", "sb", "sk", "st",
            "sx0", "sx1", "ss0", "ss1", "kk0", "kk1", "kv0", "sq0", "sxsp",
            "xr0", "xr1", "xr2", "xr3", "sy0", "sy1", "w0", "w1", "w2", "w3",
            "xq0", "xq1", "x1q0", "x1q1", "sxf", "qld", "cs0", "cs1"]
    SYNC_RINGS = ("sd", "sb", "st", "sx0", "sx1", "ss0", "ss1", "kk0", "kk1",
                  "kv0", "sxsp", "xr0", "xr1", "xr2", "xr3", "sy0", "sy1", "xq0", "xq1", "x1q0", "x1q1", "sxf", "qld", "cs0", "cs1")
    GP_RINGS = ("sg", "sk", "sq0", "w0", "w1", "w2", "w3")
    handles = {s: st.enter_context(nc.semaphore(s)) for s in sems}

    nc.const_aps.register(F32, 0.0, cst[:, 0:1])
    nc.const_aps.register(F32, EPS, cst[:, 1:2])

    p = Prog()

    def chunk(c, lo=0, hi=T):
        return stream[:, c * T + lo: c * T + hi]

    def xb(c, lo=0, hi=T):
        return xbA[:, c * T + lo: c * T + hi]

    def rw(col, n):
        return rows[0:1, col:col + n]

    def bcol(c0, p0=0, p1=128):
        return biasar[p0:p1, c0:c0 + 1]

    def dma(eng, out, in_, waits=(), ring=None):
        sem = ring if ring is not None else ("sd" if eng == "sync" else "sg")
        return p.run(eng, lambda e, o=out, i=in_: e.dma_start(out=o, in_=i),
                     waits=waits, inc=sem, inck=16)

    def now_dma(eng):
        rings = SYNC_RINGS if eng == "sync" else GP_RINGS
        return tuple(p.now(s) for s in rings)

    def mm(out, lhsT, rhs, start, stop, waits=(), inc=None):
        return p.run("tensor",
                     lambda e, o=out, l=lhsT, r=rhs, s1=start, s2=stop:
                     e.matmul(o, l, r, start=s1, stop=s2, skip_group_check=True),
                     waits=waits, inc=inc)

    def act(out, in_, func, bias=0.0, scale=1.0, waits=(), inc="sa"):
        return p.run("scalar",
                     lambda e, o=out, i=in_, f=func, b=bias, s=scale:
                     e.activation(out=o, in_=i, func=f, bias=b, scale=s),
                     waits=waits, inc=inc)

    def tt(out, a, b, op, waits=(), inc="sv"):
        return p.run("vector",
                     lambda e, o=out, x=a, y=b, z=op: e.tensor_tensor(o, x, y, z),
                     waits=waits, inc=inc)

    def ts2(out, in_, s1, op0, s2=None, op1=None, waits=(), inc="sv"):
        if op1 is None:
            fn = lambda e, o=out, i=in_, a=s1, p0=op0: \
                e.tensor_scalar(out=o, in0=i, scalar1=a, scalar2=None, op0=p0)
        else:
            fn = lambda e, o=out, i=in_, a=s1, b=s2, p0=op0, p1=op1: \
                e.tensor_scalar(out=o, in0=i, scalar1=a, scalar2=b, op0=p0, op1=p1)
        return p.run("vector", fn, waits=waits, inc=inc)

    def vcopy(out, in_, waits=(), inc="sv"):
        return p.run("vector", lambda e, o=out, i=in_: e.tensor_copy(o, i),
                     waits=waits, inc=inc)

    def vrecip(out, in_, waits=()):
        return p.run("vector", lambda e, o=out, i=in_: e.reciprocal(o, i),
                     waits=waits, inc="sv")

    def gp(build_fn, waits=(), inc=None, inck=1):
        return p.run("gpsimd", build_fn, waits=waits, inc=inc, inck=inck)

    last_pad_mm = [None, None]
    def expand(row_ap, pb, n, waits=()):
        wv = vcopy(pads[0:1, 0:n], row_ap,
                   waits=tuple(waits) + (last_pad_mm[0], last_pad_mm[1]))
        tok = mm(pb[:, 0:n], e0f[:], pads[:, 0:n], True, True,
                 waits=(wv,), inc="sp")
        last_pad_mm[0] = tok
        last_pad_mm[1] = tok
        return tok

    gp(lambda e: e.memset(cst[:, 0:1], 0.0))
    gp(lambda e: e.memset(cst[:, 1:2], EPS))
    gp(lambda e: e.memset(pads[:], 0.0))
    gp(lambda e: e.memset(padb[:], 0.0))
    m1_ = gp(lambda e: e.memset(e0f[:], 0.0), inc="so", inck=1)
    m2_ = gp(lambda e: e.memset(e0b[:], 0.0), inc="so", inck=1)
    m3_ = gp(lambda e: e.memset(onespadb[:], 0.0), inc="so", inck=1)
    m4_ = gp(lambda e: e.memset(meanpadb[:], 0.0), inc="so", inck=1)
    m5_ = gp(lambda e: e.memset(identf[:], 0.0), inc="so", inck=1)
    gp(lambda e: e.affine_select(out=e0f[:], in_=e0f[:], compare_op=ALU.not_equal,
                                 fill=1.0, base=0, pattern=[[0, 128]], channel_multiplier=1),
       waits=(m5_,))
    gp(lambda e: e.affine_select(out=e0b[:], in_=e0b[:], compare_op=ALU.not_equal,
                                 fill=1.0, base=0, pattern=[[0, 128]], channel_multiplier=1),
       waits=(m2_,))
    gp(lambda e: e.affine_select(out=onespadb[:], in_=onespadb[:], compare_op=ALU.not_equal,
                                 fill=1.0, base=0, pattern=[[1, 128]], channel_multiplier=0),
       waits=(m3_,))
    gp(lambda e: e.affine_select(out=meanpadb[:], in_=meanpadb[:], compare_op=ALU.not_equal,
                                 fill=1.0 / D, base=0, pattern=[[1, 128]], channel_multiplier=0),
       waits=(m4_,))
    tok_consts = gp(lambda e: e.affine_select(out=identf[:], in_=identf[:],
                                              compare_op=ALU.not_equal,
                                              fill=1.0, base=0, pattern=[[-1, 128]],
                                              channel_multiplier=1),
                    waits=(m1_,), inc="so", inck=1)

    m6_ = gp(lambda e: e.memset(attnf[:, 0:128], 0.0), inc="so", inck=1)
    dma("sync", attnf[0:24, 0:128], dbqkv[:].rearrange("(a b) -> a b", b=128),
        waits=(m6_,), ring="sb")
    dma("sync", attnf[BC_BP:BC_BP + 8, 0:128], dbproj[:].rearrange("(a b) -> a b", b=128), ring="sb")
    dma("sync", attnf[BC_CB1:BC_CB1 + 8, 0:128], dcb1[:].rearrange("(a b) -> a b", b=128), ring="sb")
    dma("sync", attnf[BC_CB2:BC_CB2 + 4, 0:128], dcb2[:].rearrange("(a b) -> a b", b=128), ring="sb")
    dma("sync", attnf[BC_CB3:BC_CB3 + 2, 0:128], dcb3[:].rearrange("(a b) -> a b", b=128), ring="sb")
    dma("sync", attnf[BC_CB4:BC_CB4 + 1, 0:1], dcb4[:].unsqueeze(0), ring="sb")
    dma("sync", attnf[BC_GB1:BC_GB1 + 4, 0:128], dgb1[:].rearrange("(a b) -> a b", b=128), ring="sb")
    dma("sync", attnf[BC_GB2:BC_GB2 + 2, 0:128], dgb2[:].rearrange("(a b) -> a b", b=128), ring="sb")
    dma("sync", attnf[BC_GB3:BC_GB3 + 1, 0:1], dgb3[:].unsqueeze(0), ring="sb")
    t_bld = dma("sync", attnf[BC_AR:BC_AR + 1, 0:1], dar[:].unsqueeze(0), ring="sb")
    t_btr = p.run("tensor",
                  lambda e: e.transpose(PS3[:, 0:64], attnf[0:64, 0:128],
                                        identf[0:64, 0:64]),
                  waits=(t_bld, tok_consts), inc="sp")
    t_bias = act(biasar[:, 0:64], PS3[:, 0:64], AF.Copy, waits=(t_btr,))
    t_bq32 = ts2(biasar[:, 0:8], biasar[:, 0:8], 1.0 / 32.0, ALU.mult, waits=(t_bias,))

    def load_w_section(dram, cols, base, region=None, waits=(), ring="sg"):
        reg = wqA if region is None else region
        tok = None
        ncc = dram.shape[0] // 128
        for cc in range(ncc):
            tok = dma("gpsimd",
                      reg[:, base + cc * cols: base + (cc + 1) * cols],
                      dram[cc * 128:(cc + 1) * 128, :],
                      waits=waits if cc == 0 else (), ring=ring)
        return tok


    def _finish_early():
        for i in range(NTT):
            dma("sync", dy[i * 128:(i + 1) * 128, :], stream[:, 0:1024],
                waits=(p.now("sp"), p.now("sv"), p.now("sa"), p.now("sc")))
        for tok in now_dma("sync") + now_dma("gpsimd"):
            p._wait("sync", tok)
        with nc.Block() as block:
            @block.sync
            def _(e):
                p.emit(e, "sync", handles)
            @block.gpsimd
            def _(e):
                p.emit(e, "gpsimd", handles)
            @block.tensor
            def _(e):
                p.emit(e, "tensor", handles)
            @block.vector
            def _(e):
                p.emit(e, "vector", handles)
            @block.scalar
            def _(e):
                p.emit(e, "scalar", handles)
        st.close()
        return nc, tap_handles

    ev_done = {}
    t_stream = None
    for tt_i in range(NTT):
        slot = temps[:, (tt_i % 2) * 1024:(tt_i % 2) * 1024 + 1024]
        tok_load = dma("sync", slot, dx[tt_i * 128:(tt_i + 1) * 128, :],
                       waits=(ev_done.get(tt_i - 2),), ring="sx0" if tt_i % 2 == 0 else "sx1")
        for half in range(2):
            tr_last = None
            for j in range(4):
                c = half * 4 + j
                if tt_i % 2 == 0:
                    bsl = (PS1 if half == 0 else PS2)[:, j * 128:j * 128 + 128]
                else:
                    bsl = PS0[:, half * 512 + j * 128: half * 512 + j * 128 + 128]
                tr_last = p.run("tensor",
                                lambda e, o=bsl, i=slot[:, c * 128:(c + 1) * 128]:
                                e.transpose(o, i, identf[:]),
                                waits=(tok_load, tok_consts,
                                       ev_done.get((tt_i - 2, half)),
                                       t_btr if tt_i < 2 else None),
                                inc="sp")
            for j in range(4):
                c = half * 4 + j
                if tt_i % 2 == 0:
                    bsl = (PS1 if half == 0 else PS2)[:, j * 128:j * 128 + 128]
                else:
                    bsl = PS0[:, half * 512 + j * 128: half * 512 + j * 128 + 128]
                t_stream = act(chunk(c, tt_i * 128, tt_i * 128 + 128), bsl, AF.Copy,
                               waits=(tr_last, t_bias if tt_i < 2 else None))
            ev_done[(tt_i, half)] = t_stream
        ev_done[tt_i] = t_stream

    if upto == "p1":
        return _finish_early()

    def ln_full(out_mode, guard, pre_cast=None):
        if guard is None:
            gt = ()
        elif isinstance(guard[0], str):
            gt = (guard,)
        else:
            gt = guard
        if pre_cast is not None:
            cast_toks = pre_cast
        else:
            toks = [vcopy(xb(c), chunk(c), waits=gt) for c in range(CH)]
            cast_toks = {(tg, c): toks[c] for tg in range(TGN) for c in range(CH)}
        sqmm_tok = {}
        e1_tok = {}
        var_tok = {}
        out_toks = []
        for tg in range(TGN):
            lo, hi = tg * TGS, (tg + 1) * TGS
            m1 = m2 = None
            for c in range(CH):
                sslot = stgA[:, (c % 4) * 512:(c % 4) * 512 + 512]
                tok_sq = act(sslot, xb(c, lo, hi), AF.Square,
                             waits=(cast_toks[(tg, c)],) + gt +
                                   (sqmm_tok.get((tg, c - 4)), sqmm_tok.get((tg - 1, c + 4))))
                m1 = mm(PS1[:, 0:512], meanpadb[:], xb(c, lo, hi), c == 0, c == CH - 1,
                        waits=(cast_toks[(tg, c)], e1_tok.get(tg - 1)) + gt,
                        inc="sp")
                m2 = mm(PS2[:, 0:512], meanpadb[:], sslot, c == 0, c == CH - 1,
                        waits=(tok_sq, var_tok.get(tg - 1)), inc="sp")
                sqmm_tok[(tg, c)] = m2
            e1 = act(rw(RW_M, TGS), PS1[0:1, 0:512], AF.Copy, waits=(m1,))
            e1_tok[tg] = e1
            v1 = tt(rw(RW_V, TGS), rw(RW_M, TGS), rw(RW_M, TGS), ALU.mult, waits=(e1,))
            v2 = tt(rw(RW_V, TGS), PS2[0:1, 0:512], rw(RW_V, TGS), ALU.subtract,
                    waits=(v1, m2))
            var_tok[tg] = v2
            v3 = act(rw(RW_V, TGS), rw(RW_V, TGS), AF.Ln, bias=EPS, waits=(v2,))
            v4 = act(rw(RW_A, TGS), rw(RW_V, TGS), AF.Exp, scale=-0.5, waits=(v3,))
            v5 = tt(rw(RW_B, TGS), rw(RW_M, TGS), rw(RW_A, TGS), ALU.mult, waits=(v4,))
            prev = tuple(out_toks[-CH:])
            ta = expand(rw(RW_A, TGS), PS3, TGS, waits=(v4,) + prev)
            tb = expand(rw(RW_B, TGS), PS4, TGS, waits=(v5,) + prev)
            for c in range(CH):
                t1 = tt(temps[:, 1536:2048], xb(c, lo, hi), PS3[:, 0:TGS], ALU.mult,
                        waits=(ta, cast_toks[(tg, c)], p.now("sv")))
                dst = xb(c, lo, hi) if out_mode == "xb" else chunk(c, lo, hi)
                out_toks.append(tt(dst, temps[:, 1536:2048], PS4[:, 0:TGS],
                                   ALU.subtract, waits=(tb, t1)))
        return out_toks

    t_h1 = ln_full("xb", t_stream)[-1]

    if "h1" in taps:
        for c in range(CH):
            dma("sync", tap_handles["h1"][c * 128:(c + 1) * 128, :],
                xbA[:].rearrange("p (c t) -> p c t", c=CH)[:, c, :], waits=(t_h1,), ring="st")

    if upto == "ln1":
        return _finish_early()

    def fm_proj_phase(base, bias_col, scale, dst_dram, w_tok, mode, bank_guard,
                      prev_dma=None):
        ev = {}
        sq_act = {}
        sq_mm = {}
        dma_toks = {}

        def emit_sq(i):
            et_, tg_ = i // TGN, i % TGN
            sslot = stgA[:, 512 + (i % 2) * 1024: 512 + (i % 2) * 1024 + 512]
            stg_ = stgA[:, (i % 2) * 1024:(i % 2) * 1024 + 512]
            sq_act[i] = act(sslot, stg_, AF.Square,
                            waits=(ev[i], sq_mm.get(i - 2)))
            sq_mm[i] = mm(PS0[:, tg_ * 512: tg_ * 512 + 512], onespadb[:], sslot,
                          et_ == 0, et_ == CH - 1,
                          waits=(sq_act[i],) + ((bank_guard,) if i < TGN else ()),
                          inc="sp")

        idx = 0
        for et in range(CH):
            for tg in range(TGN):
                lo, hi = tg * TGS, (tg + 1) * TGS
                bank = PS1 if idx % 2 == 0 else PS2
                last = None
                for cc in range(CH):
                    w = (w_tok, t_h1, ev.get(idx - 2), bank_guard if idx == 0 else None)
                    last = mm(bank[:, 0:512],
                              wqA[:, base + cc * D + et * 128: base + cc * D + et * 128 + 128],
                              xb(cc, lo, hi), cc == 0, cc == CH - 1,
                              waits=w if cc == 0 else (), inc="sp" if cc == CH - 1 else None)
                stg = stgA[:, (idx % 2) * 1024:(idx % 2) * 1024 + 512]
                ev[idx] = act(stg, bank[:, 0:512], AF.Identity,
                              bias=bcol(bias_col + et), scale=scale,
                              waits=(last, t_bq32,
                                     dma_toks.get(idx - 2,
                                                  prev_dma[idx % 2] if prev_dma else None),
                                     sq_act.get(idx - 2)))
                dma_toks[idx] = dma("sync", dst_dram[et * 128:(et + 1) * 128, lo:hi],
                                    stg, waits=(ev[idx],),
                                    ring="ss0" if idx % 2 == 0 else "ss1")
                if idx >= 1:
                    emit_sq(idx - 1)
                idx += 1
        emit_sq(idx - 1)
        row_toks = []
        for tg in range(TGN):
            wmm = (sq_mm[(CH - 1) * TGN + tg],)
            if mode == "ksq":
                row_toks.append(p.run(
                    "vector",
                    lambda e, o=rw(RW_MISC + tg, 1), i=PS0[0:1, tg * 512: tg * 512 + 512]:
                    e.reduce_max(o, i, axis=AX.X),
                    waits=wmm, inc="sv"))
            else:
                row_toks.append(act(qsqB[0:1, tg * TGS:(tg + 1) * TGS],
                                    PS0[0:1, tg * 512: tg * 512 + 512], AF.Copy,
                                    waits=wmm))
        return ev[idx - 1], (dma_toks[idx - 2], dma_toks[idx - 1]), row_toks

    t_wk = load_w_section(dwqkv[:, D:2 * D], D, 0, waits=(tok_consts,), ring="w0")
    t_wv = load_w_section(dwqkv[:, 2 * D:3 * D], D, 8192, waits=(tok_consts,), ring="w1")
    k_ev, k_dmas, k_sq = fm_proj_phase(0, BC_BK, 1.0, kag_in, t_wk, "ksq", t_stream)
    k_dma = k_dmas[1]
    k_sp = p.now("sp")

    t_kmax = p.run("vector",
                   lambda e: e.reduce_max(rw(RW_MISC + 4, 1), rw(RW_MISC, 4), axis=AX.X),
                   waits=tuple(k_sq), inc="sv")
    t_kb = vcopy(rw(RW_MISC + 8, 8), rw(RW_MISC + 4, 1).to_broadcast((1, 8)),
                 waits=(t_kmax,))
    t_kd = dma("gpsimd", kmax_in[:], rw(RW_MISC + 8, 8), waits=(t_kb,), ring="sk")

    v_ev = {}
    v_dma = {}
    idx = 0
    for tt_i in range(NTT):
        for eg in range(2):
            bank = PS1 if idx % 2 == 0 else PS2
            last = None
            for cc in range(CH):
                w = (t_wv, t_h1, v_ev.get(idx - 2), k_ev if idx == 0 else None)
                last = mm(bank[:, 0:512],
                          xb(cc, tt_i * 128, (tt_i + 1) * 128),
                          wqA[:, 8192 + cc * D + eg * 512: 8192 + cc * D + eg * 512 + 512],
                          cc == 0, cc == CH - 1,
                          waits=w if cc == 0 else (), inc="sp" if cc == CH - 1 else None)
            stg = stgA[:, (idx % 2) * 1024:(idx % 2) * 1024 + 512]
            v_ev[idx] = act(stg, bank[:, 0:512], AF.Copy,
                            waits=(last, v_dma.get(idx - 2, k_dmas[idx % 2])))
            v_dma[idx] = dma("sync",
                             vag_in[tt_i * 128:(tt_i + 1) * 128, eg * 512:(eg + 1) * 512],
                             stg, waits=(v_ev[idx],),
                             ring="ss0" if idx % 2 == 0 else "ss1")
            idx += 1
    v_last_ev = v_ev[idx - 1]
    v_last_dma = v_dma[idx - 1]

    if skip_cc:
        t_ar_kmax = dma("gpsimd", kmax_out[:], kmax_in[:], waits=(t_kd,))
    else:
        t_ar_kmax = gp(lambda e: e.collective_compute(
            "AllReduce", ALU.max, replica_groups=[[0, 1, 2, 3], [4, 5, 6, 7]],
            ins=[kmax_in[:]], outs=[kmax_out[:]]),
            waits=(t_kd, k_dmas[0], k_dmas[1]), inc="sc")
    t_kmg = dma("gpsimd", rw(RW_MISC + 16, 8), kmax_out[:], waits=(t_ar_kmax,), ring="sk")

    if skip_cc:
        t_ag_k = dma("gpsimd", kag_out_r(0), kag_in, waits=(k_dma,))
        t_ag_v = dma("gpsimd", vag_out_r(0), vag_in, waits=(v_last_dma,))
    else:
        t_ag_k = gp(lambda e: e.collective_compute(
            "AllGather", ALU.bypass, replica_groups=[[0, 1, 2, 3], [4, 5, 6, 7]],
            ins=[kvag_in_t[:]], outs=[kvag_out_t[:]]),
            waits=(t_kmg, v_dma[idx - 2], v_last_dma), inc="sc")
        t_ag_v = t_ag_k
    if "kag" in taps:
        dma("sync", tap_handles["kag"][:], kag_in, waits=(k_dma, t_ag_k), ring="st")
    if "vag" in taps:
        dma("sync", tap_handles["vag"][:], vag_in, waits=(v_last_dma, t_ag_v), ring="st")

    t_wq = load_w_section(dwqkv[:, 0:D], D, 0, waits=(k_sp,), ring="w2")
    q_ev, q_dmas, q_sq = fm_proj_phase(0, BC_BQ, 1.0 / 32.0, qt_dram, t_wq, "qsq",
                                      v_last_ev, prev_dma=(v_dma[idx - 2], v_dma[idx - 1]))
    q_dma = q_dmas[1]
    if "qt" in taps:
        dma("sync", tap_handles["qt"][:], qt_dram[:], waits=(q_dma,), ring="st")

    t_wproj = load_w_section(dwproj, D, 8192, waits=(p.now("sp"),), ring="w3")
    t_cn1 = load_w_section(dcw1, D, 0, waits=(q_dma, p.now("sp")), ring="w0")

    if upto == "p3":
        return _finish_early()

    xf_sp = None
    for c in range(CH):
        xf_sp = dma("sync", xfm_dram[c * 128:(c + 1) * 128, :], chunk(c),
                    waits=(t_h1,), ring="sxf")

    q_ld = None
    for cc in range(CH):
        q_ld = dma("sync", xbA[:, cc * T:(cc + 1) * T],
                   qt_dram[cc * 128:(cc + 1) * 128, :],
                   waits=(q_dmas[0], q_dmas[1], q_ev), ring="qld")

    def qT(cc, qlo, qhi):
        return xbA[:, cc * T + qlo: cc * T + qhi]

    def emb_sl(qt, lo=0, hi=8 * QTS):
        return stream[:, qt * (CH * QTS) + lo: qt * (CH * QTS) + hi]

    sb1 = tt(rw(0, 1024), qsqB[0:1, 0:1024],
             rw(RW_MISC + 16, 1).to_broadcast((1, 1024)), ALU.mult,
             waits=(t_kmg, q_sq[-1], p.now("sv")))
    sb1b = tt(rw(1024, 1024), qsqB[0:1, 1024:2048],
              rw(RW_MISC + 16, 1).to_broadcast((1, 1024)), ALU.mult, waits=(sb1,))
    sb2 = act(rw(0, 1024), rw(0, 1024), AF.Ln, waits=(sb1,))
    sb2 = act(rw(0, 1024), rw(0, 1024), AF.Exp, scale=0.5, waits=(sb2,))
    sb2b = act(rw(1024, 1024), rw(1024, 1024), AF.Ln, waits=(sb1b,))
    sb2b = act(rw(1024, 1024), rw(1024, 1024), AF.Exp, scale=0.5, waits=(sb2b,))
    sb3 = ts2(rw(0, 1024), rw(0, 1024), -1.01, ALU.mult, waits=(sb2,))
    sb3b = ts2(rw(1024, 1024), rw(1024, 1024), -1.01, ALU.mult, waits=(sb2b,))
    t_pad = vcopy(padb[0:1, 0:1024], rw(0, 1024), waits=(sb3,))
    t_pad = vcopy(padb[0:1, 1024:2048], rw(1024, 1024), waits=(sb3b, t_pad))

    def sums_sl(qt):
        return rows[0:1, qt * QTS:(qt + 1) * QTS]

    use_k = {}
    use_v = {}
    exp_guard = {}
    evac_tok = {}
    sums_tok = {}
    for kb in range(NKB):
        r, ts_off = kb // 4, (kb % 4) * KBS
        par = kb % 2
        kslot = kvA[:, par * 4096: par * 4096 + 4096]
        vslot = kvA[:, 8192:12288]
        tk = dma("sync", kslot.rearrange("p (cc t) -> p cc t", cc=CH),
                 kag_out_r(r).rearrange("(cc p) t -> p cc t", p=128)[:, :, ts_off:ts_off + KBS],
                 waits=(t_ag_v, use_k.get(kb - 2)),
                 ring="kk0" if kb % 2 == 0 else "kk1")
        tv = dma("sync", vslot.rearrange("p (kt d) -> p kt d", kt=4),
                 vag_out_r(r)[ts_off:ts_off + KBS, :].rearrange("(kt p) d -> p kt d", p=128),
                 waits=(t_ag_v, use_v.get(kb - 1)), ring="kv0")
        for qt_i in range(NQT):
            qlo = qt_i * QTS
            exp_toks = []
            stab_tok = {}
            for kt in range(4):
                bank = PS1 if kt < 2 else PS2
                boff = (kt % 2) * 256
                w0 = (tk, q_ld, t_pad, exp_guard.get(kt // 2))
                for cc in range(CH):
                    mm(bank[:, boff:boff + 256],
                       kvA[:, par * 4096 + cc * 512 + kt * 128: par * 4096 + cc * 512 + kt * 128 + 128],
                       qT(cc, qlo, qlo + QTS),
                       cc == 0, False, waits=w0 if cc == 0 else ())
                stab_tok[kt] = mm(bank[:, boff:boff + 256], e0b[:],
                                  padb[:, qlo:qlo + QTS], False, True, inc="sp")
                if kt % 2 == 1:
                    dst = prA[:, (kt // 2) * 512: (kt // 2) * 512 + 512]
                    etok = act(dst, bank[:, 0:512], AF.Exp,
                               waits=(stab_tok[kt - 1], stab_tok[kt]))
                    exp_toks.append(etok)
                    exp_guard[kt // 2] = etok
            if qt_i == NQT - 1:
                use_k[kb] = stab_tok[3]
            av_last = None
            for dt in range(CH):
                for kt in range(4):
                    pr = prA[:, kt * 256: kt * 256 + 256]
                    w = ()
                    if kt == 0:
                        w = (tv, exp_toks[0], evac_tok.get((kb, qt_i - 1) if qt_i > 0 else (kb - 1, NQT - 1)))
                    elif kt == 2:
                        w = (exp_toks[1],)
                    av_last = mm(PS0[:, dt * 256: dt * 256 + 256],
                                 kvA[:, 8192 + kt * 1024 + dt * 128: 8192 + kt * 1024 + dt * 128 + 128],
                                 pr, (kt == 0 and dt % 2 == 0), False,
                                 waits=w, inc="sp" if kt == 3 and dt == CH - 1 else None)
            sums_last = None
            for kt in range(4):
                pr = prA[:, kt * 256: kt * 256 + 256]
                w = ()
                if kt == 0:
                    w = (exp_toks[0], sums_tok.get((kb, qt_i - 1) if qt_i > 0 else (kb - 1, NQT - 1)))
                elif kt == 2:
                    w = (exp_toks[1],)
                sums_last = mm(PS3[:, 0:256], onespadb[:], pr,
                               kt == 0, kt == 3,
                               waits=w, inc="sp" if kt == 3 else None)
            if qt_i == NQT - 1:
                use_v[kb] = av_last
            if kb == 0:
                evac_tok[(kb, qt_i)] = vcopy(emb_sl(qt_i), PS0[:, 0:2048],
                                             waits=(av_last, xf_sp, p.now("sv")))
                sums_tok[(kb, qt_i)] = vcopy(sums_sl(qt_i), PS3[0:1, 0:256],
                                             waits=(sums_last,))
            else:
                evac_tok[(kb, qt_i)] = tt(emb_sl(qt_i), PS0[:, 0:2048], emb_sl(qt_i),
                                          ALU.add, waits=(av_last, p.now("sv")))
                sums_tok[(kb, qt_i)] = tt(sums_sl(qt_i), PS3[0:1, 0:256], sums_sl(qt_i),
                                          ALU.add, waits=(sums_last,))

    x1_last = {}
    t1_done = {}
    k_end = evac_tok[(NKB - 1, NQT - 1)]
    s_end = sums_tok[(NKB - 1, NQT - 1)]
    bgq = {"R0": k_end, "R1": k_end, "A0": k_end, "A1": k_end,
           "B0": s_end, "B1": None, "PS1": None, "PS2": None, "emA": None}
    def xdst(qt, lo=0, hi=8 * QTS):
        if qt == 0:
            return temps[:, lo:hi]
        return emb_sl(qt - 1, lo, hi)
    for qt_i in range(NQT):
        qlo = qt_i * QTS
        xload = None
        for c in range(CH):
            xload = dma("sync", xdst(qt_i, c * QTS, (c + 1) * QTS),
                        xfm_dram[c * 128:(c + 1) * 128, qlo:qlo + QTS],
                        waits=(xf_sp, t1_done.get(qt_i - 1),
                               x1_last.get(qt_i - 2)),
                        ring="xq0" if qt_i % 2 == 0 else "xq1")
        pad_off = (qt_i % 2) * 256
        bcR = PS0[:, (qt_i % 2) * 512: (qt_i % 2) * 512 + QTS]
        bcA = PS0[:, 1024 + (qt_i % 2) * 512: 1024 + (qt_i % 2) * 512 + QTS]
        bcB = PS3[:, 0:QTS] if qt_i % 2 == 0 else PS4[:, 0:QTS]
        wv_l = act(sums_sl(qt_i), sums_sl(qt_i), AF.Ln,
                   waits=(sums_tok[(NKB - 1, qt_i)],))
        wv_r = act(pads[0:1, pad_off:pad_off + QTS], sums_sl(qt_i), AF.Exp,
                   scale=-1.0, waits=(wv_l, last_pad_mm[qt_i % 2]))
        t_bc = mm(bcR, e0f[:], pads[:, pad_off:pad_off + QTS], True, True,
                  waits=(wv_r, bgq.get("R%d" % (qt_i % 2))), inc="sp")
        last_pad_mm[qt_i % 2] = t_bc
        emb_toks = []
        t1_last = None
        for dt in range(CH):
            t1_last = tt(attnf[:, dt * 256:(dt + 1) * 256],
                         emb_sl(qt_i, dt * 256, dt * 256 + 256), bcR,
                         ALU.mult,
                         waits=(t_bc, p.now("sv"),
                                bgq.get("apG") if dt == 0 else None))
            emb_toks.append(act(emA[:, dt * 256:(dt + 1) * 256],
                                attnf[:, dt * 256:(dt + 1) * 256], AF.Identity,
                                bias=bcol(BC_BV + dt),
                                waits=(t1_last, bgq.get("emA"))))
        proj_toks = []
        for et in range(CH):
            bank = PS1 if et % 2 == 0 else PS2
            last = None
            for cc in range(CH):
                w = (t_wproj, emb_toks[-1],
                     proj_toks[et - 2] if et >= 2 else
                     (exp_guard.get(et) if qt_i == 0 else bgq.get("PS1" if et == 0 else "PS2")))
                last = mm(bank[:, 0:256],
                          wqA[:, 8192 + cc * D + et * 128: 8192 + cc * D + et * 128 + 128],
                          emA[:, cc * 256:(cc + 1) * 256],
                          cc == 0, cc == CH - 1,
                          waits=w if cc == 0 else (), inc="sp" if cc == CH - 1 else None)
            proj_toks.append(act(attnf[:, et * 256:(et + 1) * 256], bank[:, 0:256],
                                 AF.Identity, bias=bcol(BC_BP + et),
                                 waits=(last, t1_last if et < 2 else None)))
        t_cast = vcopy(stgA[:, 0:2048].rearrange("p (a b) -> p a b", a=1) if False else stgA[:, 0:2048],
                       attnf[:], waits=(proj_toks[-1], p.now("ss0"), p.now("ss1")))
        t_sq = act(emA[:, 0:2048], attnf[:], AF.Square,
                   waits=(proj_toks[-1], emb_toks[-1]))
        m1 = m2 = None
        for cc in range(CH):
            m1 = mm(PS1[:, 0:256], meanpadb[:], stgA[:, cc * 256:(cc + 1) * 256],
                    cc == 0, cc == CH - 1,
                    waits=(t_cast, proj_toks[-2]) if cc == 0 else (),
                    inc="sp" if cc == CH - 1 else None)
        for cc in range(CH):
            m2 = mm(PS2[:, 0:256], meanpadb[:], emA[:, cc * 256:(cc + 1) * 256],
                    cc == 0, cc == CH - 1,
                    waits=(t_sq, proj_toks[-1]) if cc == 0 else (),
                    inc="sp" if cc == CH - 1 else None)
        e1 = act(rw(RW2_M, QTS), PS1[0:1, 0:256], AF.Copy, waits=(m1,))
        v1 = tt(rw(RW2_V, QTS), rw(RW2_M, QTS), rw(RW2_M, QTS), ALU.mult, waits=(e1,))
        v2 = tt(rw(RW2_V, QTS), PS2[0:1, 0:256], rw(RW2_V, QTS), ALU.subtract,
                waits=(v1, m2))
        v3 = act(rw(RW2_V, QTS), rw(RW2_V, QTS), AF.Ln, bias=EPS, waits=(v2,))
        v4 = act(rw(RW2_A, QTS), rw(RW2_V, QTS), AF.Exp, scale=-0.5, waits=(v3,))
        v5 = tt(rw(RW2_B, QTS), rw(RW2_M, QTS), rw(RW2_A, QTS), ALU.mult, waits=(v4,))
        ta = mm(bcA, e0f[:], pads[:, pad_off:pad_off + QTS], True, True,
                waits=(vcopy(pads[0:1, pad_off:pad_off + QTS], rw(RW2_A, QTS),
                             waits=(v4, last_pad_mm[qt_i % 2])),
                       bgq.get("A%d" % (qt_i % 2))), inc="sp")
        last_pad_mm[qt_i % 2] = ta
        tb = mm(bcB, e0f[:], pads[:, pad_off:pad_off + QTS], True, True,
                waits=(vcopy(pads[0:1, pad_off:pad_off + QTS], rw(RW2_B, QTS),
                             waits=(v5, last_pad_mm[qt_i % 2])),
                       bgq.get("B%d" % (qt_i % 2))), inc="sp")
        last_pad_mm[qt_i % 2] = tb
        ap = None
        u1_latest = None
        for cc in range(CH):
            u1_latest = tt(attnf[:, cc * 256:(cc + 1) * 256],
                           attnf[:, cc * 256:(cc + 1) * 256], bcA,
                           ALU.mult, waits=(ta, t_cast, t_sq, p.now("sv")))
            u2 = tt(attnf[:, cc * 256:(cc + 1) * 256],
                    attnf[:, cc * 256:(cc + 1) * 256], bcB,
                    ALU.subtract, waits=(tb, u1_latest))
            ap = p.run("gpsimd",
                       lambda e, o=xdst(qt_i, cc * QTS, (cc + 1) * QTS),
                       a=attnf[:, cc * 256:(cc + 1) * 256]:
                       e.tensor_tensor(o, o, a, ALU.add),
                       waits=(u2, xload), inc="so")
        x1d = None
        for c in range(CH):
            x1d = dma("sync", x1_dram[c * 128:(c + 1) * 128, qlo:qlo + QTS],
                      xdst(qt_i, c * QTS, (c + 1) * QTS), waits=(ap,),
                      ring="x1q0" if qt_i % 2 == 0 else "x1q1")
        x1_last[qt_i] = x1d
        t1_done[qt_i] = t1_last
        bgq["PS1"] = e1
        bgq["PS2"] = v2
        bgq["emA"] = m2
        bgq["R%d" % (qt_i % 2)] = t1_last
        bgq["A%d" % (qt_i % 2)] = u1_latest
        bgq["B%d" % (qt_i % 2)] = u2
        bgq["apG"] = ap

    t_x1a = t_x1b = None
    for c in range(CH):
        tok = dma("sync", chunk(c), x1_dram[c * 128:(c + 1) * 128, :],
                  waits=(x1_last[NQT - 1], x1_last[NQT - 2],
                         evac_tok[(NKB - 1, 7)], p.now("sv")),
                  ring="sx0" if c % 2 == 0 else "sx1")
    t_x1a = p.now("sx0")
    t_x1b = p.now("sx1")
    t_x1 = (t_x1a, t_x1b)
    x1_spill = x1_last[NQT - 1]

    if "x1" in taps:
        for c in range(CH):
            dma("sync", tap_handles["x1"][c * 128:(c + 1) * 128, :], chunk(c),
                waits=t_x1, ring="st")

    if upto == "attn":
        return _finish_early()

    ln3 = ln_full("xb", t_x1)
    ln3_tg = [ln3[tg * CH + CH - 1] for tg in range(TGN)]
    t_h2 = ln3[-1]
    if "h2" in taps:
        for c in range(CH):
            dma("sync", tap_handles["h2"][c * 128:(c + 1) * 128, :],
                xbA[:].rearrange("p (c t) -> p c t", c=CH)[:, c, :], waits=(t_h2,), ring="st")

    ctx_tok = None
    for c in range(CH):
        ctx_tok = p.run("vector",
                        lambda e, o=ctxP[:, c:c + 1], i=xb(c):
                        e.reduce_sum(o, i, axis=AX.X),
                        waits=(t_h2,), inc="sv")
    t_ctxd = dma("gpsimd", ctx_in[:], ctxP[:, 0:CH], waits=(ctx_tok,), ring="sk")
    if skip_cc:
        t_ar_ctx = dma("gpsimd", ctx_out[:], ctx_in[:], waits=(t_ctxd,))
    else:
        t_ar_ctx = gp(lambda e: e.collective_compute(
            "AllReduce", ALU.add, replica_groups=[[0, 1, 2, 3], [4, 5, 6, 7]],
            ins=[ctx_in[:]], outs=[ctx_out[:]]),
        waits=(t_ctxd, x1_spill) + now_dma("sync") + now_dma("gpsimd"), inc="sc")
    t_ctxl = dma("gpsimd", ctxP[:, 8:16], ctx_out[:], waits=(t_ar_ctx,), ring="sk")
    t_ctxs = ts2(ctxP[:, 8:16], ctxP[:, 8:16], 1.0 / S, ALU.mult, waits=(t_ctxl,))
    if "ctx" in taps:
        dma("sync", tap_handles["ctx"][:], ctxP[:, 8:16], waits=(t_ctxs,), ring="st")
    ctxb = stgA[:, 0:8]
    t_ctxb = vcopy(ctxb, ctxP[:, 8:16], waits=(t_ctxs, p.now("sp")))

    if upto == "ln3":
        return _finish_early()

    t_cn2 = load_w_section(dcw2, 512, 8192, waits=(p.now("sp"),), ring="w1")
    t_cn3 = load_w_section(dcw3, 256, 12288, waits=(t_cn2,), ring="w2")
    mz1 = gp(lambda e: e.memset(wqA[:, 13312:13568], 0.0), waits=(t_cn3,),
             inc="so", inck=1)
    t_cn4a = dma("gpsimd", wqA[:, 13312:13313], dcw4[0:128, :], waits=(mz1,), ring="w3")
    t_cn4 = dma("gpsimd", wqA[:, 13440:13441], dcw4[128:256, :], ring="w3")
    t_gc2 = load_w_section(dgw2, 256, 13568, waits=(t_cn4,), ring="w0")
    mz2 = gp(lambda e: e.memset(wqA[:, 14592:14848], 0.0), waits=(t_gc2,),
             inc="so", inck=1)
    t_gc3a = dma("gpsimd", wqA[:, 14592:14593], dgw3[0:128, :], waits=(mz2,), ring="w1")
    t_gc3 = dma("gpsimd", wqA[:, 14720:14721], dgw3[128:256, :], ring="w1")
    t_gc1 = load_w_section(dgw1, 512, 7168, region=kvA, waits=(p.now("sp"),), ring="w2")
    t_wffn = (p.now("w0"), p.now("w1"), p.now("w2"), p.now("w3"))

    gc_toks = {}
    def emit_gc():
        gc1_toks = []
        for et in range(4):
            last = None
            for cc in range(CH):
                last = mm(PS0[:, et * 512:et * 512 + 1],
                          kvA[:, 7168 + cc * 512 + et * 128: 7168 + cc * 512 + et * 128 + 128],
                          ctxb[:, cc:cc + 1], cc == 0, cc == CH - 1,
                          waits=t_wffn + (t_ctxb,) if cc == 0 else (),
                          inc="sp" if cc == CH - 1 else None)
            gc1_toks.append(act(emA[:, 1024 + et:1024 + et + 1], PS0[:, et * 512:et * 512 + 1],
                                AF.Gelu, bias=bcol(BC_GB1 + et), waits=(last,)))
        gc2_toks = []
        for et in range(2):
            last = None
            for cc in range(4):
                last = mm(PS0[:, et * 512 + 256:et * 512 + 257],
                          wqA[:, 13568 + cc * 256 + et * 128: 13568 + cc * 256 + et * 128 + 128],
                          emA[:, 1024 + cc:1024 + cc + 1], cc == 0, cc == 3,
                          waits=t_wffn + (gc1_toks[-1],) if cc == 0 else (),
                          inc="sp" if cc == 3 else None)
            gc2_toks.append(act(emA[:, 1032 + et:1032 + et + 1], PS0[:, et * 512 + 256:et * 512 + 257],
                                AF.Gelu, bias=bcol(BC_GB2 + et), waits=(last,)))
        last = None
        for cc in range(2):
            last = mm(PS0[:, 1024:1025], wqA[:, 14592 + cc * 128:14592 + cc * 128 + 128],
                      emA[:, 1032 + cc:1032 + cc + 1], cc == 0, cc == 1,
                      waits=t_wffn + (gc2_toks[-1],) if cc == 0 else (),
                      inc="sp" if cc == 1 else None)
        t_gf = act(bcol(BC_GF, 0, 1), PS0[0:1, 1024:1025], AF.Sigmoid,
                   bias=bcol(BC_GB3, 0, 1), waits=(last,))
        s1 = ts2(bcol(BC_T1, 0, 1), bcol(BC_GF, 0, 1), 1.4, ALU.mult, waits=(t_gf,))
        s2 = tt(bcol(BC_T1, 0, 1), bcol(BC_T1, 0, 1), bcol(BC_AR, 0, 1), ALU.mult,
                waits=(s1, t_bias))
        s3 = tt(bcol(BC_T1, 0, 1), bcol(BC_T1, 0, 1), bcol(BC_AR, 0, 1), ALU.subtract,
                waits=(s2,))
        s4 = ts2(bcol(BC_SADD, 0, 1), bcol(BC_T1, 0, 1), 0.1, ALU.add, waits=(s3,))
        t_smul = ts2(bcol(BC_SMUL, 0, 1), bcol(BC_AR, 0, 1), 0.6, ALU.mult, waits=(t_bias,))

        gc_toks['s4'] = s4
        gc_toks['smul'] = t_smul

    cyc_done = t_h2
    ln4_casts = {}
    for cyc in range(3):
        if cyc > 0:
            src_tok_pair = (p.now("cs0"), p.now("cs1"))
        else:
            src_tok_pair = ()
        hp = {}
        apply_done = {}
        evt_toks = {}

        def emit_apply(tg, cyc=cyc, hp=hp, apply_done=apply_done, evt_toks=evt_toks,
                       src_tok_pair=src_tok_pair):
            lo, hi = tg * TGS, (tg + 1) * TGS
            c1 = ts2(rw(RW_STG, TGS), tfB[0:1, lo:hi], bcol(BC_SMUL, 0, 1), ALU.mult,
                     bcol(BC_SADD, 0, 1), ALU.add,
                     waits=(gc_toks['s4'], gc_toks['smul'], evt_toks[tg]))
            prev = tuple(apply_done.get((tg - 1, c)) for c in range(CH))
            tc = expand(rw(RW_STG, TGS), PS3, TGS, waits=(c1,) + prev)
            c2 = ts2(rw(RW_STG, TGS), rw(RW_STG, TGS), -1.0, ALU.mult, waits=(tc,))
            tn = expand(rw(RW_STG, TGS), PS4, TGS, waits=(c2,) + prev)
            last = None
            for c in range(CH):
                src_c, sign = TWIST[c]
                bcp, bct = (PS3, tc) if sign > 0 else (PS4, tn)
                u1 = tt(temps[:, 1536:2048], xb(src_c, lo, hi), bcp[:, 0:TGS],
                        ALU.mult, waits=(bct, p.now("sv")) + src_tok_pair)
                base = xb(c, lo, hi) if cyc == 0 else chunk(c, lo, hi)
                last = tt(chunk(c, lo, hi), base, temps[:, 1536:2048], ALU.add,
                          waits=(u1, x1_spill if cyc == 0 else None))
                apply_done[(tg, c)] = last
            return last

        for tg in range(TGN):
            lo, hi = tg * TGS, (tg + 1) * TGS
            h1_toks = []
            for et in range(CH):
                bank = PS1 if et % 2 == 0 else PS2
                last = None
                for cc in range(CH):
                    w = t_wffn + src_tok_pair + (
                         ln3_tg[tg] if cyc == 0 else None,
                         hp.get(("h1", tg, et - 2)),
                         hp.get(("h4", tg - 1)) if et < 2 else None)
                    last = mm(bank[:, 0:512],
                              wqA[:, cc * D + et * 128: cc * D + et * 128 + 128],
                              xb(cc, lo, hi), cc == 0, cc == CH - 1,
                              waits=w if cc == 0 else (),
                              inc="sp" if cc == CH - 1 else None)
                ev = act(kvA[:, et * 512:(et + 1) * 512], bank[:, 0:512], AF.Gelu,
                         bias=bcol(BC_CB1 + et), waits=(last,))
                hp[("h1", tg, et)] = ev
                h1_toks.append(ev)
            if cyc == 0 and tg == 0:
                emit_gc()
            h2_toks = []
            for et in range(4):
                bank = PS1 if et % 2 == 0 else PS2
                last = None
                for cc in range(CH):
                    w = t_wffn + (h1_toks[-1], hp.get(("h2", tg, et - 2)))
                    last = mm(bank[:, 0:512],
                              wqA[:, 8192 + cc * 512 + et * 128: 8192 + cc * 512 + et * 128 + 128],
                              kvA[:, cc * 512:(cc + 1) * 512], cc == 0, cc == CH - 1,
                              waits=w if cc == 0 else (),
                              inc="sp" if cc == CH - 1 else None)
                ev = act(kvA[:, 4096 + et * 512: 4096 + (et + 1) * 512], bank[:, 0:512],
                         AF.Gelu, bias=bcol(BC_CB2 + et), waits=(last,))
                hp[("h2", tg, et)] = ev
                h2_toks.append(ev)
            h3_toks = []
            for et in range(2):
                bank = PS1 if et % 2 == 0 else PS2
                last = None
                for cc in range(4):
                    w = t_wffn + (h2_toks[-1], hp.get(("h3", tg, et - 2)))
                    last = mm(bank[:, 0:512],
                              wqA[:, 12288 + cc * 256 + et * 128: 12288 + cc * 256 + et * 128 + 128],
                              kvA[:, 4096 + cc * 512: 4096 + (cc + 1) * 512],
                              cc == 0, cc == 3,
                              waits=w if cc == 0 else (), inc="sp" if cc == 3 else None)
                ev = act(kvA[:, 6144 + et * 512: 6144 + (et + 1) * 512], bank[:, 0:512],
                         AF.Gelu, bias=bcol(BC_CB3 + et), waits=(last,))
                hp[("h3", tg, et)] = ev
                h3_toks.append(ev)
            last = None
            for cc in range(2):
                w = t_wffn + (h3_toks[0], h3_toks[1], hp.get(("h4", tg - 1)))
                last = mm(PS1[:, 0:512],
                          wqA[:, 13312 + cc * 128: 13312 + cc * 128 + 128],
                          kvA[:, 6144 + cc * 512: 6144 + (cc + 1) * 512],
                          cc == 0, cc == 1,
                          waits=w if cc == 0 else (), inc="sp" if cc == 1 else None)
            evt = act(tfB[0:1, lo:hi], PS1[0:1, 0:512], AF.Sigmoid,
                      bias=bcol(BC_CB4, 0, 1), waits=(last,))
            hp[("h4", tg)] = evt
            evt_toks[tg] = evt
            if tg > 0:
                ap_tok = emit_apply(tg - 1)
                lo1, hi1 = (tg - 1) * TGS, tg * TGS
                if cyc == 2:
                    for c in range(CH):
                        ln4_casts[(tg - 1, c)] = vcopy(
                            xb(c, lo1, hi1), chunk(c, lo1, hi1), waits=(ap_tok,))
                else:
                    for c in range(CH):
                        dma("gpsimd", xb(c, lo1, hi1), chunk(c, lo1, hi1),
                            waits=(ap_tok,), ring="cs0" if c % 2 == 0 else "cs1")
        cyc_done = emit_apply(TGN - 1)
        lo1, hi1 = (TGN - 1) * TGS, TGN * TGS
        if cyc == 2:
            for c in range(CH):
                ln4_casts[(TGN - 1, c)] = vcopy(
                    xb(c, lo1, hi1), chunk(c, lo1, hi1), waits=(cyc_done,))
        else:
            for c in range(CH):
                dma("gpsimd", xb(c, lo1, hi1), chunk(c, lo1, hi1),
                    waits=(cyc_done,), ring="cs0" if c % 2 == 0 else "cs1")

    if "ffn" in taps:
        for c in range(CH):
            dma("sync", tap_handles["ffn"][c * 128:(c + 1) * 128, :], chunk(c),
                waits=(cyc_done,), ring="st")

    if upto == "ffn":
        return _finish_early()

    ln4 = ln_full("stream", cyc_done, pre_cast=ln4_casts)
    t_z = ln4[-1]
    add_toks = []
    tg_done = {}
    y_prev = {}
    ydma_prev = {}
    ti = 0
    for tg in range(TGN):
        lo, hi = tg * TGS, (tg + 1) * TGS
        for c in range(CH):
            slot = attnf[:, (c % 4) * 512:(c % 4) * 512 + 512]
            tl = dma("sync", slot, x1_dram[c * 128:(c + 1) * 128, lo:hi],
                     waits=(x1_spill, t_z,
                            add_toks[-4] if len(add_toks) >= 4 else None),
                     ring="xr%d" % (c % 4))
            add_toks.append(tt(chunk(c, lo, hi), chunk(c, lo, hi), slot, ALU.add,
                               waits=(tl, t_z)))
        tg_done[tg] = add_toks[-1]
        for tt_j in range(4):
            tt_i = tg * 4 + tt_j
            ystg = temps[:, (tt_i % 2) * 1024:(tt_i % 2) * 1024 + 1024]
            evs = []
            for half in range(2):
                tr_last = None
                for j in range(4):
                    c = half * 4 + j
                    if tt_i % 2 == 0:
                        bsl = (PS1 if half == 0 else PS2)[:, j * 128:j * 128 + 128]
                    else:
                        bsl = PS0[:, half * 512 + j * 128: half * 512 + j * 128 + 128]
                    tr_last = p.run("tensor",
                                    lambda e, o=bsl, i=chunk(c, tt_i * 128, (tt_i + 1) * 128):
                                    e.transpose(o, i, identf[:]),
                                    waits=(tg_done[tg], y_prev.get((tt_i - 2, half))),
                                    inc="sp")
                for j in range(4):
                    c = half * 4 + j
                    if tt_i % 2 == 0:
                        bsl = (PS1 if half == 0 else PS2)[:, j * 128:j * 128 + 128]
                    else:
                        bsl = PS0[:, half * 512 + j * 128: half * 512 + j * 128 + 128]
                    ev = act(ystg[:, c * 128:(c + 1) * 128], bsl, AF.Copy,
                             waits=(tr_last, ydma_prev.get(tt_i - 2)))
                    evs.append(ev)
                y_prev[(tt_i, half)] = evs[-1]
            ydma_prev[tt_i] = dma("sync" if tt_i % 2 == 0 else "gpsimd",
                                  dy[tt_i * 128:(tt_i + 1) * 128, :], ystg,
                                  waits=tuple(evs[-8:]),
                                  ring="sy0" if tt_i % 2 == 0 else "cs0")
            ti += 1

    for tok in now_dma("sync") + now_dma("gpsimd"):
        p._wait("sync", tok)

    with nc.Block() as block:
        @block.sync
        def _(e):
            p.emit(e, "sync", handles)

        @block.gpsimd
        def _(e):
            p.emit(e, "gpsimd", handles)

        @block.tensor
        def _(e):
            p.emit(e, "tensor", handles)

        @block.vector
        def _(e):
            p.emit(e, "vector", handles)

        @block.scalar
        def _(e):
            p.emit(e, "scalar", handles)

    st.close()
    return nc, tap_handles


_CACHE = {}

W_KEYS = {
    "w_qkv": "L0_attn_w_qkv", "b_qkv": "L0_attn_b_qkv",
    "w_proj": "L0_attn_w_proj", "b_proj": "L0_attn_b_proj",
    "cn_w1": "L0_ffn_cn_w1", "cn_b1": "L0_ffn_cn_b1",
    "cn_w2": "L0_ffn_cn_w2", "cn_b2": "L0_ffn_cn_b2",
    "cn_w3": "L0_ffn_cn_w3", "cn_b3": "L0_ffn_cn_b3",
    "cn_w4": "L0_ffn_cn_w4", "cn_b4": "L0_ffn_cn_b4",
    "gc_w1": "L0_ffn_gc_w1", "gc_b1": "L0_ffn_gc_b1",
    "gc_w2": "L0_ffn_gc_w2", "gc_b2": "L0_ffn_gc_b2",
    "gc_w3": "L0_ffn_gc_w3", "gc_b3": "L0_ffn_gc_b3",
    "adaptive_range": "L0_ffn_adaptive_range",
}


def kernel(**inputs):
    x = np.ascontiguousarray(np.asarray(inputs["L0_attn_x"], dtype=np.float32))
    weights = {k: np.ascontiguousarray(np.asarray(inputs[v], dtype=np.float32))
               for k, v in W_KEYS.items()}
    if "nc" not in _CACHE:
        _CACHE["nc"] = build()[0]
    nc = _CACHE["nc"]
    in_maps = []
    for core in range(8):
        b, g = core // 4, core % 4
        m = dict(weights)
        m["x"] = np.ascontiguousarray(x[b, g * T:(g + 1) * T, :])
        in_maps.append(m)
    res = run_bass_kernel_spmd(nc, in_maps, core_ids=list(range(8)))
    y = np.empty((2, S, D), dtype=np.float32)
    for core in range(8):
        b, g = core // 4, core % 4
        y[b, g * T:(g + 1) * T, :] = res.results[core]["y"]
    return y
